# version 31
# baseline (speedup 1.0000x reference)
import contextlib
import numpy as np
import concourse.bass as bass
import concourse.mybir as mybir
from concourse.bass_utils import run_bass_kernel_spmd

F32 = mybir.dt.float32
BF16 = mybir.dt.bfloat16
AF = mybir.ActivationFunctionType
ALU = mybir.AluOpType
AX = mybir.AxisListType

HID = 1024
HEADS = 16
HD = 64
SPLIT_TR = ()
NCHUNK = 8
N_SEQ = 2
EPS_LN = 1e-5
GATE_EPS = 1e-4
W_THRES = 0.1
NEXP = 8


class Ctx:
    ENGMAP = {"T": "tensor", "V": "vector", "S": "scalar", "G": "gpsimd", "Y": "sync"}

    def __init__(self, nc, block, stack):
        self.nc = nc
        self.block = block
        self.stack = stack
        self.sems = {}
        self.counts = {}
        self.lastw = {}
        self.readers = {}
        self.water = {}
        self.batch = set()
        self.upgrades = {}
        self.pend_eng = None
        self.pend = []
        for e in self.ENGMAP:
            self._mksem("e:" + e)

    def _mksem(self, name):
        h = self.stack.enter_context(self.nc.semaphore(name.replace(":", "_")))
        self.sems[name] = h
        self.counts[name] = 0

    def dsem(self, name, batch=False):
        key = "d:" + name
        if key not in self.sems:
            self._mksem(key)
        if batch:
            self.batch.add(key)
        return key

    def _flush(self):
        if self.pend:
            fns = self.pend
            self.pend = []
            getattr(self.block, self.ENGMAP[self.pend_eng])(
                lambda e, fns=fns: [f(e) for f in fns])

    def _emit(self, eng, fn):
        if self.pend_eng != eng:
            self._flush()
            self.pend_eng = eng
        self.pend.append(fn)

    def _wait(self, eng, dep):
        s, v = dep
        if s.startswith("d:"):
            if s in self.batch:
                v = self.counts[s]
            elif v != self.counts[s]:
                self.upgrades[s] = self.upgrades.get(s, 0) + 1
                v = self.counts[s]
        if v <= self.water.get((eng, s), 0):
            return
        self.water[(eng, s)] = v
        self._emit(eng, lambda e, s=s, v=v: e.wait_ge(self.sems[s], v))

    def _deps(self, reads, writes):
        deps = {}

        def add(t):
            if t is None:
                return
            if isinstance(t, list):
                for tt in t:
                    add(tt)
                return
            deps[t[0]] = max(deps.get(t[0], 0), t[1])

        for k in reads:
            add(self.lastw.get(k))
        for k in writes:
            for t in self.readers.get(k, ()):
                add(t)
            add(self.lastw.get(k))
        return sorted(deps.items())

    def fence_keys(self, keys, sems):
        toks = [(s, self.counts[s]) for s in sems if self.counts.get(s, 0) > 0]
        for k in keys:
            self.lastw[k] = list(toks)

    def _book(self, tok, reads, writes):
        for k in writes:
            self.lastw[k] = tok
            self.readers[k] = []
        for k in reads:
            self.readers.setdefault(k, []).append(tok)
        return tok

    def op(self, eng, fn, reads=(), writes=()):
        for d in self._deps(reads, writes):
            self._wait(eng, d)
        sem = "e:" + eng
        self.counts[sem] += 1
        v = self.counts[sem]
        self._emit(eng, lambda e, fn=fn, sem=sem, v=v:
                   fn(e).then_inc(self.sems[sem], 1))
        return self._book((sem, v), reads, writes)

    def dma(self, eng, dsem_name, fn, reads=(), writes=(), batch=False):
        sem = self.dsem(dsem_name, batch=batch)
        for d in self._deps(reads, writes):
            self._wait(eng, d)
        self.counts[sem] += 16
        v = self.counts[sem]
        self._emit(eng, lambda e, fn=fn, sem=sem, v=v:
                   fn(e).then_inc(self.sems[sem], 16))
        return self._book((sem, v), reads, writes)

    def wait_all_dma(self, eng):
        for s, v in list(self.counts.items()):
            if s.startswith("d:") and v > 0:
                self._wait(eng, (s, v))

    def barrier(self):
        for eng in self.ENGMAP:
            for s, v in list(self.counts.items()):
                if v > 0 and s != "e:" + eng:
                    self._wait(eng, (s, v))
        self.lastw.clear()
        self.readers.clear()

    def finish(self):
        self.wait_all_dma("G")
        self.wait_all_dma("S")
        self.wait_all_dma("Y")
        self._flush()


def build_program(seq: int):
    NT = seq // 128
    T16 = seq // 16
    HPR = 128 // T16 if T16 <= 128 else 0
    assert T16 <= 128 and 128 % T16 == 0
    GT = min(4, NT) if seq >= 2048 else min(2, NT)
    assert NT % GT == 0
    NG = NT // GT
    GS = 128 * GT
    NB2 = 2 if seq >= 2048 else 1
    NU = N_SEQ * NT

    nc = bass.Bass("TRN2", target_bir_lowering=False, debug=False,
                   dynamic_dma_scratch_size=16384)

    dx = nc.dram_tensor("x", [N_SEQ, seq, HID], F32, kind="ExternalInput").ap()
    dwq = nc.dram_tensor("wq", [HID, HID], F32, kind="ExternalInput").ap()
    dwk = nc.dram_tensor("wk", [HID, HID], F32, kind="ExternalInput").ap()
    dwvt = nc.dram_tensor("wvt", [HID, HID], F32, kind="ExternalInput").ap()
    dwo = nc.dram_tensor("wo", [HID, HID], F32, kind="ExternalInput").ap()
    dbo = nc.dram_tensor("bo", [1, HID], F32, kind="ExternalInput").ap()
    dgw = nc.dram_tensor("gw", [HID, NEXP], F32, kind="ExternalInput").ap()
    dgbr = nc.dram_tensor("gbr", [128, 2 * NEXP], F32, kind="ExternalInput").ap()
    dexp = nc.dram_tensor("expw", [NEXP, HID, HID], F32, kind="ExternalInput").ap()
    dball = nc.dram_tensor("ball", [NEXP, HID], F32, kind="ExternalInput").ap()
    dct = nc.dram_tensor("ct", [HID, N_SEQ], F32, kind="ExternalInput").ap()
    dwmod = nc.dram_tensor("wmod", [HID, 6 * HID], F32, kind="ExternalInput").ap()
    did = nc.dram_tensor("ident", [128, 128], F32, kind="ExternalInput").ap()
    attscr = nc.dram_tensor("attscr", [16, 16], BF16).ap()
    dout = nc.dram_tensor("out", [N_SEQ, seq, HID], F32, kind="ExternalOutput").ap()

    h1scr = nc.dram_tensor("h1scr", [N_SEQ, seq, HID], BF16).ap()
    vscr = nc.dram_tensor("vscr", [seq, HID], BF16).ap()
    wvscr = nc.dram_tensor("wvscr", [HID, HID], BF16).ap()
    x1scr = nc.dram_tensor("x1scr", [N_SEQ * seq, HID], F32).ap()
    h2scr = nc.dram_tensor("h2scr", [N_SEQ * seq, HID], BF16).ap()
    modscr = nc.dram_tensor("modscr", [16, HID], F32).ap()
    expscr = nc.dram_tensor("expscr", [NEXP, HID, HID], BF16).ap()

    st = contextlib.ExitStack()
    E = st.enter_context

    psA = E(nc.psum_tensor("psA", [128, 1024], F32))
    psB = E(nc.psum_tensor("psB", [128, 1024], F32))
    psC = E(nc.psum_tensor("psC", [128, 1024], F32))
    psE = E(nc.psum_tensor("psE", [128, 128], F32))
    psX = E(nc.psum_tensor("psX", [128, 256], BF16))




    ones1 = E(nc.sbuf_tensor("ones1", [1, 128], BF16))
    gbrep = E(nc.sbuf_tensor("gbrep", [128, 2 * NEXP], F32))
    ballp = [E(nc.sbuf_tensor(f"ballp{i}", [128, HID], BF16)) for i in range(2)]
    onest = E(nc.sbuf_tensor("onest", [128, 128], BF16))
    gatew = E(nc.sbuf_tensor("gatew", [128, NCHUNK, NEXP], BF16))
    borow_bf = E(nc.sbuf_tensor("borow_bf", [1, HID], BF16))
    ctsb = E(nc.sbuf_tensor("ctsb", [128, NCHUNK, N_SEQ], F32))
    csil = E(nc.sbuf_tensor("csil", [128, NCHUNK, N_SEQ], BF16))

    gsc = E(nc.sbuf_tensor("gsc", [128, 56], F32))
    ffst = E(nc.sbuf_tensor("ffst", [128, NU, 4], F32))
    lnst = E(nc.sbuf_tensor("lnst", [128, NT, 4], F32))

    att_bf = E(nc.sbuf_tensor("att_bf", [16, 16], BF16))


    ctx = Ctx(nc, E(nc.Block()), st)

    def vop(fn, reads=(), writes=()):
        return ctx.op("V", fn, reads, writes)

    def sop(fn, reads=(), writes=()):
        return ctx.op("S", fn, reads, writes)

    def top(fn, reads=(), writes=()):
        return ctx.op("T", fn, reads, writes)

    xb = [E(nc.sbuf_tensor(f"xb{i}", [128, HID], F32)) for i in range(2)]
    sqb = E(nc.sbuf_tensor("sqb", [128, HID], F32))
    sqdump = E(nc.sbuf_tensor("sqdump", [128, HID], BF16))
    lnb = [E(nc.sbuf_tensor(f"lnb{i}", [128, HID], F32)) for i in range(2)]
    hb = [E(nc.sbuf_tensor(f"hb{i}", [128, HID], BF16)) for i in range(NB2)]

    ast_ = contextlib.ExitStack()
    A = ast_.enter_context
    pst = contextlib.ExitStack()
    P = pst.enter_context
    WVP = A(nc.sbuf_tensor("WVP", [128, NCHUNK, HID], BF16, side="right"))
    id128 = A(nc.sbuf_tensor("id128", [128, 128], BF16))
    id128f = A(nc.sbuf_tensor("id128f", [128, 128], F32))
    attT_sb = A(nc.sbuf_tensor("attT_sb", [16, 16], BF16))
    e16 = A(nc.sbuf_tensor("e16", [16, 64], F32))
    bd8 = A(nc.sbuf_tensor("bd8", [128, 128], BF16))
    WVTS = A(nc.sbuf_tensor("WVTS", [128, NCHUNK, HID], BF16, side="right"))
    rowbuf = P(nc.sbuf_tensor("rowbuf", [N_SEQ, HID], F32))
    rowtmp = P(nc.sbuf_tensor("rowtmp", [N_SEQ, HID], F32))

    ctx.dma("G", "cst", lambda e: e.dma_start(id128[:], did), writes=["id128"], batch=True)
    ctx.dma("G", "cst", lambda e: e.dma_start(id128f[:], did), writes=["id128f"], batch=True)
    ctx.dma("G", "cst", lambda e: e.dma_start(gbrep[:], dgbr), writes=["gbrep"], batch=True)
    for e_i in range(NEXP):
        ctx.dma("G", "cst", lambda e, e_i=e_i: e.dma_start(
            ballp[e_i // 4][32 * (e_i % 4):32 * (e_i % 4) + 1, :],
            dball[e_i:e_i + 1, :]),
            writes=[f"ballp{e_i // 4}"], batch=True)
    ctx.dma("G", "cst", lambda e: e.dma_start(
        gatew[:], dgw.rearrange("(k p) e -> p k e", p=128)), writes=["gatew"], batch=True)
    ctx.dma("G", "cst", lambda e: e.dma_start(borow_bf[:], dbo),
            writes=["borow_bf"], batch=True)
    ctx.dma("Y", "cstY", lambda e: e.dma_start(
        ctsb[:], dct.rearrange("(k p) n -> p k n", p=128)), writes=["ct"], batch=True)
    vop(lambda e: e.memset(ones1[:], 1.0), writes=["ones1"])
    vop(lambda e: e.memset(onest[:], 1.0), writes=["onest"])

    sop(lambda e: e.activation(csil[:], ctsb[:], AF.Silu),
        reads=["ct"], writes=["csil"])


    def emit_mod(jbs, ldb, ldk, rb, rbk, rt_, rtk, stage=None):
        rot = [(psA, "psA"), (psB, "psB"), (psC, "psC")]
        for ji, jb in enumerate(jbs):
            ps, pk = rot[ji % 3]
            if stage is not None:
                sbuf3, skey = stage[ji]
                ctx.dma("G", f"wms{ji}", lambda e, jb=jb, sbuf3=sbuf3: e.dma_start(
                    sbuf3[:], dwmod[:, HID * jb:HID * (jb + 1)]
                    .rearrange("(k p) f -> p k f", p=128)),
                    writes=[skey] + [f"{skey}:{k}" for k in range(NCHUNK)])
                for k in range(NCHUNK):
                    for h in range(2):
                        wkey = [f"{pk}:{h}"] if k in (0, NCHUNK - 1) else []
                        top(lambda e, k=k, h=h, ps=ps, sbuf3=sbuf3: e.matmul(
                            ps[0:N_SEQ, 512 * h:512 * h + 512], csil[:, k, :],
                            sbuf3[:, k, 512 * h:512 * h + 512],
                            start=(k == 0), stop=(k == NCHUNK - 1)),
                            reads=["csil", f"{skey}:{k}"], writes=wkey)
            else:
                for k in range(NCHUNK):
                    ctx.dma("G", f"wm{k % 2}", lambda e, k=k, jb=jb: e.dma_start(
                        ldb[k % len(ldb)][:],
                        dwmod[128 * k:128 * k + 128, HID * jb:HID * (jb + 1)]),
                        writes=[ldk[k % len(ldb)]])
                    for h in range(2):
                        wkey = [f"{pk}:{h}"] if k in (0, NCHUNK - 1) else []
                        top(lambda e, k=k, h=h, ps=ps: e.matmul(
                            ps[0:N_SEQ, 512 * h:512 * h + 512], csil[:, k, :],
                            ldb[k % len(ldb)][:, 512 * h:512 * h + 512],
                            start=(k == 0), stop=(k == NCHUNK - 1)),
                            reads=["csil", ldk[k % len(ldb)]], writes=wkey)
            vop(lambda e, ps=ps: e.tensor_copy(rb[0:N_SEQ, :], ps[0:N_SEQ, :]),
                reads=[f"{pk}:0", f"{pk}:1"], writes=[rbk])
            if jb in (1, 4):
                vop(lambda e: e.tensor_scalar_add(rb[0:N_SEQ, :], rb[0:N_SEQ, :], 1.0),
                    reads=[rbk], writes=[rbk])
            elif jb in (2, 5):
                vop(lambda e: e.tensor_scalar(rt_[0:N_SEQ, :], rb[0:N_SEQ, :],
                                              0.0, None, ALU.is_ge),
                    reads=[rbk], writes=[rtk])
                vop(lambda e: e.tensor_scalar(rt_[0:N_SEQ, :], rt_[0:N_SEQ, :],
                                              2.0, -1.0, ALU.mult, ALU.add),
                    reads=[rtk], writes=[rtk])
                vop(lambda e: e.tensor_mul(rb[0:N_SEQ, :], rb[0:N_SEQ, :],
                                           rt_[0:N_SEQ, :]),
                    reads=[rbk, rtk], writes=[rbk])
                vop(lambda e: e.tensor_scalar_max(rb[0:N_SEQ, :],
                                                  rb[0:N_SEQ, :], 0.25),
                    reads=[rbk], writes=[rbk])
                vop(lambda e: e.tensor_mul(rb[0:N_SEQ, :], rb[0:N_SEQ, :],
                                           rt_[0:N_SEQ, :]),
                    reads=[rbk, rtk], writes=[rbk])
            for n in range(N_SEQ):
                ctx.dma("G", "modw", lambda e, n=n, jb=jb: e.dma_start(
                    modscr[6 * n + jb:6 * n + jb + 1, :], rb[n:n + 1, :]),
                    reads=[rbk], writes=[f"modscr:{n}:{jb}"], batch=True)

    emit_mod([0, 1], None, None, rowbuf, "rowbuf",
             rowtmp, "rowtmp", stage=[(WVP, "WVPg"), (WVTS, "wvts_stage")])

    pst.close()
    ctx.fence_keys(["xb0", "xb1", "sqb", "lnb0", "lnb1", "hb0", "hb1"],
                   ["e:T", "e:V", "e:G", "d:wm0", "d:wm1", "d:modw"])

    WQ = A(nc.sbuf_tensor("WQ", [128, NCHUNK, HID], BF16, side="right"))
    WK = A(nc.sbuf_tensor("WK", [128, NCHUNK, HID], BF16, side="right"))
    WO = A(nc.sbuf_tensor("WO", [128, NCHUNK, HID], BF16, side="right"))
    H1T = A(nc.sbuf_tensor("H1T", [128, NCHUNK, seq], BF16, side="right"))
    VPT = A(nc.sbuf_tensor("VPT", [128, NCHUNK, seq], BF16, side="right"))

    for nm, sb, src in (("WQ", WQ, dwq), ("WK", WK, dwk)):
        ctx.dma("G", "wl", lambda e, sb=sb, src=src: e.dma_start(
            sb[:], src.rearrange("(k p) f -> p k f", p=128)),
            writes=[f"{nm}:{k}" for k in range(NCHUNK)], batch=True)

    qsb = [A(nc.sbuf_tensor(f"qsb{i}", [128, HID], BF16, side="right")) for i in range(1)]
    ksb = [A(nc.sbuf_tensor(f"ksb{i}", [128, HID], BF16, side="right")) for i in range(1)]
    vtm = [A(nc.sbuf_tensor(f"vtm{i}", [128, HID], BF16, side="right")) for i in range(1)]
    scrT = [A(nc.sbuf_tensor(f"scrT{i}", [128, NCHUNK, 128], BF16, side="right")) for i in range(1)]
    x1b = [A(nc.sbuf_tensor(f"x1b{i}", [128, HID], F32, side="right")) for i in range(NB2)]
    esb = A(nc.sbuf_tensor("esb", [128, 128], F32, side="right"))
    ca1rep = A(nc.sbuf_tensor("ca1rep", [128, HID], BF16))
    sarep = A(nc.sbuf_tensor("sarep", [128, HID], BF16))
    garep = A(nc.sbuf_tensor("garep", [128, HID], BF16))

    def emit_ffn_stats(xbuf, xkey, u):
        c = ffst[:, u, :]
        k = f"ffr:{u}"
        vop(lambda e, c=c: e.reduce_sum(c[:, 0:1], xbuf[:], axis=AX.X),
            reads=[xkey], writes=[k])
        vop(lambda e, c=c: e.scalar_tensor_tensor(
            sqdump[:], xbuf[:], 1.0, xbuf[:], ALU.mult, ALU.mult,
            accum_out=c[:, 1:2]),
            reads=[xkey], writes=[k, "sqdump"])

    def emit_rstd_batch(F, rk, k):
        vop(lambda e: e.tensor_scalar_mul(F[:, :, 0:1], F[:, :, 0:1], 1.0 / HID),
            reads=rk, writes=[k + ":mean"])
        vop(lambda e: e.tensor_mul(F[:, :, 2:3], F[:, :, 0:1], F[:, :, 0:1]),
            reads=[k + ":mean"], writes=[k + ":msq"])
        vop(lambda e: e.tensor_scalar(F[:, :, 1:2], F[:, :, 1:2], 1.0 / HID,
                                      EPS_LN, ALU.mult, ALU.add),
            reads=rk, writes=[k + ":v1"])
        vop(lambda e: e.tensor_sub(F[:, :, 1:2], F[:, :, 1:2], F[:, :, 2:3]),
            reads=[k + ":v1", k + ":msq"], writes=[k + ":var"])
        vop(lambda e: e.reciprocal(F[:, :, 3:4], F[:, :, 1:2]),
            reads=[k + ":var"], writes=[k + ":rvar"])
        sop(lambda e: e.activation(F[:, :, 3:4], F[:, :, 3:4], AF.Sqrt),
            reads=[k + ":rvar"], writes=[k + ":rstd"])
        vop(lambda e: e.scalar_tensor_tensor(
            F[:, :, 2:3], F[:, :, 0:1], -1.0, F[:, :, 3:4], ALU.mult, ALU.mult),
            reads=[k + ":mean", k + ":rstd"], writes=[k])

    def emit_ffn_batch(n):
        emit_rstd_batch(ffst[:, n * NT:(n + 1) * NT, :],
                        [f"ffr:{u}" for u in range(n * NT, (n + 1) * NT)],
                        f"ffb:{n}")

    xc = A(nc.sbuf_tensor("xc", [128, HID], F32, side="right"))
    exstage = None

    def attn_bc(n):
        for rep, key, slot in ((ca1rep, "ca1rep", 1), (sarep, "sarep", 0)):
            ctx.dma("G", "bc" + key, lambda e, rep=rep, slot=slot, n=n: e.dma_start(
                rep[:],
                modscr[6 * n + slot:6 * n + slot + 1, :].to_broadcast([128, HID])),
                reads=[f"modscr:{n}:{slot}"], writes=[key])

    def a_fe_pre(n, t):
        bi = t % NB2
        ctx.dma("Y", f"xs{bi}", lambda e, t=t, bi=bi, n=n: e.dma_start(
            xb[bi][:], dx[n, 128 * t:128 * t + 128, :]),
            writes=[f"xb{bi}"])
        c = lnst[:, t, :]
        vop(lambda e, c=c, bi=bi: e.reduce_sum(c[:, 0:1], xb[bi][:], axis=AX.X),
            reads=[f"xb{bi}"], writes=[f"lnr:{t}"])
        vop(lambda e, c=c, bi=bi: e.scalar_tensor_tensor(
            sqdump[:], xb[bi][:], 1.0, xb[bi][:], ALU.mult, ALU.mult,
            accum_out=c[:, 1:2]),
            reads=[f"xb{bi}"], writes=[f"lnr:{t}", "sqdump"])

    def a_fe_post(n, t, t0):
        bi = t % NB2
        vop(lambda e, t=t, bi=bi: e.tensor_scalar(
            lnb[t % 2][:], xb[bi][:], lnst[:, t, 3:4], lnst[:, t, 2:3],
            ALU.mult, ALU.add),
            reads=[f"xb{bi}", f"lnp:{t0}"], writes=[f"lnb{t % 2}"])
        vop(lambda e, t=t: e.tensor_mul(sqb[:], lnb[t % 2][:], ca1rep[:]),
            reads=[f"lnb{t % 2}", "ca1rep"], writes=["sqb"])
        vop(lambda e, bi=bi: e.tensor_add(hb[bi][:], sqb[:], sarep[:]),
            reads=["sqb", "sarep"], writes=[f"hb{bi}"])
        ctx.dma("Y", f"h1w{bi}", lambda e, t=t, bi=bi, n=n: e.dma_start(
            h1scr[n, 128 * t:128 * t + 128, :], hb[bi][:]),
            reads=[f"hb{bi}"], writes=[f"h1s:{n}:{t}"])

    PB = NB2

    def a_fe(n, t):
        a_fe_pre(n, t)
        if (t + 1) % PB == 0:
            t0 = t - PB + 1
            emit_rstd_batch(lnst[:, t0:t0 + PB, :],
                            [f"lnr:{tt}" for tt in range(t0, t + 1)],
                            f"lnp:{t0}")
            for tt in range(t0, t + 1):
                a_fe_post(n, tt, t0)

    def a_rd(n, gg):
        for k in range(NCHUNK):
            ctx.dma("S", f"h1r{k}",
                    lambda e, k=k, gg=gg, n=n: e.dma_start(
                        H1T[:, k, GS * gg:GS * gg + GS],
                        h1scr[n, GS * gg:GS * gg + GS, 128 * k:128 * k + 128],
                        transpose=True),
                    reads=[f"h1s:{n}:{tt}"
                           for tt in range(GT * gg, GT * (gg + 1))],
                    writes=[f"h1Tg:{gg}:{k}"])

    def a_mm(n, t, qrot):
        gg = t // GT
        psQ, qk = ((psA, "psA") if (t % 2 == 0 or not qrot) else (psC, "psC"))
        for k in range(NCHUNK):
            for h in range(2):
                wb = [f"psB:{h}"] if k in (0, NCHUNK - 1) else []
                top(lambda e, k=k, h=h, t=t: e.matmul(
                    psB[:, 512 * h:512 * h + 512],
                    H1T[:, k, 128 * t:128 * t + 128],
                    WK[:, k, 512 * h:512 * h + 512],
                    start=(k == 0), stop=(k == NCHUNK - 1)),
                    reads=[f"h1Tg:{gg}:{k}", f"WK:{k}"], writes=wb)
        for k in range(NCHUNK):
            for h in range(2):
                wa = [f"{qk}:{h}"] if k in (0, NCHUNK - 1) else []
                top(lambda e, k=k, h=h, t=t, psQ=psQ: e.matmul(
                    psQ[:, 512 * h:512 * h + 512],
                    H1T[:, k, 128 * t:128 * t + 128],
                    WQ[:, k, 512 * h:512 * h + 512],
                    start=(k == 0), stop=(k == NCHUNK - 1)),
                    reads=[f"h1Tg:{gg}:{k}", f"WQ:{k}"], writes=wa)
        vop(lambda e: e.tensor_copy(ksb[0][:], psB[:]),
            reads=["psB:0", "psB:1"], writes=["ksb0"])
        vop(lambda e, psQ=psQ: e.tensor_copy(qsb[0][:], psQ[:]),
            reads=[f"{qk}:0", f"{qk}:1"], writes=["qsb0"])
        for blk in range(8):
            first = (t == 0 and blk == 0)
            last = (t == NT - 1 and blk == 7)
            top(lambda e, blk=blk, first=first, last=last: e.matmul(
                psE[:, :], qsb[0][:, 128 * blk:128 * blk + 128],
                ksb[0][:, 128 * blk:128 * blk + 128],
                start=first, stop=last),
                reads=["qsb0", "ksb0"],
                writes=(["psEacc"] if (first or last) else []))

    def mid(n):
        vop(lambda e: e.tensor_copy(esb[:], psE[:]),
            reads=["psEacc"], writes=["esb"])
        for dl in range(8):
            pk = "psC:0" if dl < 4 else "psC:1"
            top(lambda e, dl=dl: e.matmul(
                psC[0:16, 128 * dl:128 * dl + 128],
                id128f[:, 16 * dl:16 * dl + 16], esb[:],
                start=True, stop=True),
                reads=["esb", "id128f"], writes=[pk])
        vop(lambda e: e.tensor_copy(e16[0:16, 0:16], psC[0:16, 0:16]),
            reads=["psC:0"], writes=["e16"])
        for dl in range(1, 8):
            off = 128 * dl + 16 * dl
            vop(lambda e, off=off: e.tensor_add(
                e16[0:16, 0:16], e16[0:16, 0:16], psC[0:16, off:off + 16]),
                reads=["psC:0", "psC:1", "e16"], writes=["e16"])
        vop(lambda e: e.tensor_scalar_mul(e16[0:16, 0:16], e16[0:16, 0:16], 0.125),
            reads=["e16"], writes=["e16"])
        vop(lambda e: e.reduce_max(e16[0:16, 16:17], e16[0:16, 0:16], axis=AX.X,
                                   negate=True),
            reads=["e16"], writes=["e16nm"])
        vop(lambda e: e.tensor_scalar(e16[0:16, 32:48], e16[0:16, 0:16],
                                      e16[0:16, 16:17], None, ALU.add),
            reads=["e16", "e16nm"], writes=["e16s"])
        sop(lambda e: e.activation(e16[0:16, 32:48], e16[0:16, 32:48], AF.Exp),
            reads=["e16s"], writes=["e16s"])
        vop(lambda e: e.reduce_sum(e16[0:16, 17:18], e16[0:16, 32:48], axis=AX.X),
            reads=["e16s"], writes=["e16sum"])
        vop(lambda e: e.reciprocal(e16[0:16, 18:19], e16[0:16, 17:18]),
            reads=["e16sum"], writes=["e16rcp"])
        vop(lambda e: e.tensor_scalar(att_bf[:, :], e16[0:16, 32:48],
                                      e16[0:16, 18:19], None, ALU.mult),
            reads=["e16s", "e16rcp"], writes=["att_bf"])
        top(lambda e: e.transpose(psX[0:16, 0:16], att_bf[:, :],
                                  id128[0:16, 0:16]),
            reads=["att_bf", "id128"], writes=["psX:a"])
        vop(lambda e: e.tensor_copy(attT_sb[:], psX[0:16, 0:16]),
            reads=["psX:a"], writes=["attT_sb"])
        ctx.dma("G", "bdw", lambda e: e.dma_start(attscr, attT_sb[:]),
                reads=["attT_sb"], writes=["attscr"], batch=True)
        vop(lambda e: e.memset(bd8[:], 0.0), writes=["bd8"])
        for dl in range(8):
            ctx.dma("G", "bdr", lambda e, dl=dl: e.dma_start(
                bd8[16 * dl:16 * dl + 16, 16 * dl:16 * dl + 16], attscr),
                reads=["attscr"], writes=["bd8"], batch=True)
        for gph in range(NCHUNK):
            for h in range(2):
                top(lambda e, gph=gph, h=h: e.matmul(
                    psC[:, 512 * h:512 * h + 512], bd8[:],
                    WVTS[:, gph, 512 * h:512 * h + 512], start=True, stop=True),
                    reads=["bd8", f"WVTS:{gph}"], writes=[f"psC:{h}"])
            vop(lambda e, gph=gph: e.tensor_copy(WVTS[:, gph, :], psC[:]),
                reads=["psC:0", "psC:1"], writes=[f"WVTS:{gph}"])
            ctx.dma("G", "wvw", lambda e, gph=gph: e.dma_start(
                wvscr.rearrange("(a gg dl) f -> gg dl a f", a=16, gg=8)[gph],
                WVTS[:, gph, :]),
                reads=[f"WVTS:{gph}"], writes=[f"wvscr:{gph}"])
        for k in range(NCHUNK):
            ctx.dma("S", f"wvr{k}", lambda e, k=k: e.dma_start(
                WVP[:, k, :], wvscr[:, 128 * k:128 * k + 128], transpose=True),
                reads=[f"wvscr:{g}" for g in range(NCHUNK)],
                writes=[f"WVPg:{k}"])

    def b1_tile(n, t):
        bi = t % NB2
        gg = t // GT
        psV, vk = (psA, "psA") if t % 2 == 0 else (psB, "psB")
        for k in range(NCHUNK):
            for h in range(2):
                wkey = [f"{vk}:{h}"] if k in (0, NCHUNK - 1) else []
                top(lambda e, k=k, h=h, t=t, psV=psV: e.matmul(
                    psV[:, 512 * h:512 * h + 512],
                    H1T[:, k, 128 * t:128 * t + 128],
                    WVP[:, k, 512 * h:512 * h + 512],
                    start=(k == 0), stop=(k == NCHUNK - 1)),
                    reads=[f"h1Tg:{gg}:{k}", f"WVPg:{k}"],
                    writes=wkey)
        vop(lambda e, psV=psV: e.tensor_copy(vtm[0][:], psV[:]),
            reads=[f"{vk}:0", f"{vk}:1"], writes=["vtm0"])
        ctx.dma("Y", f"vw{t % 2}", lambda e, t=t: e.dma_start(
            vscr.rearrange("(j tg) f -> tg j f", j=16)[8 * t:8 * t + 8, :, :],
            vtm[0][:]),
            reads=["vtm0"], writes=[f"vscr:{t}"])

    def b1_vtr(n):
        for c in range(NCHUNK):
            ctx.dma("S", f"vtr{c}", lambda e, c=c: e.dma_start(
                VPT[:, c, :], vscr[:, 128 * c:128 * c + 128], transpose=True),
                reads=[f"vscr:{t}" for t in range(NT)],
                writes=[f"VPTg:{c}"])

    def b2_tile(n, rt, psO, ok, xbuf, xk, xsem):
        bi = rt % NB2
        for al in range(HPR):
            a = rt * HPR + al
            c = a // 2
            po = (a % 2) * 64
            srcv = VPT[po:po + 64, c, :].rearrange(
                "p (mj jl tg) -> p mj jl tg", mj=8, jl=2)
            for jl in range(2):
                vop(lambda e, srcv=srcv, jl=jl, al=al: e.tensor_copy(
                    scrT[0][64 * jl:64 * jl + 64, :,
                            T16 * al:T16 * al + T16],
                    srcv[:, :, jl, :]),
                    reads=[f"VPTg:{a // 2}"], writes=["scrT0"])
        for m in range(NCHUNK):
            for h in range(2):
                wkey = [f"{ok}:{h}"] if m == 0 else []
                top(lambda e, m=m, h=h, psO=psO: e.matmul(
                    psO[:, 512 * h:512 * h + 512], scrT[0][:, m, :],
                    WO[:, m, 512 * h:512 * h + 512],
                    start=(m == 0), stop=False),
                    reads=["scrT0", f"WO:{m}"], writes=wkey)
        for h in range(2):
            top(lambda e, h=h, psO=psO: e.matmul(
                psO[:, 512 * h:512 * h + 512], ones1[:],
                borow_bf[:, 512 * h:512 * h + 512], start=False, stop=True),
                reads=["ones1", "borow_bf"], writes=[f"{ok}:{h}"])
        ctx.dma("Y", xsem, lambda e, rt=rt, n=n, xbuf=xbuf: e.dma_start(
            xbuf[:], dx[n, 128 * rt:128 * rt + 128, :]),
            writes=[xk])
        vop(lambda e, bi=bi, psO=psO: e.tensor_mul(x1b[bi][:], psO[:],
                                                   garep[:]),
            reads=[f"{ok}:0", f"{ok}:1", "garep"], writes=[f"x1b{bi}"])
        vop(lambda e, bi=bi, xbuf=xbuf: e.tensor_add(x1b[bi][:], x1b[bi][:],
                                                     xbuf[:]),
            reads=[f"x1b{bi}", xk], writes=[f"x1b{bi}"])
        emit_ffn_stats(x1b[bi], f"x1b{bi}", n * NT + rt)
        ctx.dma("Y", f"x1w{bi}", lambda e, rt=rt, bi=bi, n=n: e.dma_start(
            x1scr[seq * n + 128 * rt: seq * n + 128 * rt + 128, :],
            x1b[bi][:]),
            reads=[f"x1b{bi}"], writes=[f"x1s:{n}:{rt}"])
        ctx.dma("Y", f"xow{bi}", lambda e, rt=rt, bi=bi, n=n: e.dma_start(
            dout[n, 128 * rt:128 * rt + 128, :], x1b[bi][:]),
            reads=[f"x1b{bi}"], writes=[f"outx:{n}:{rt}"])

    def bcgarep(n):
        ctx.dma("G", "bcgarep", lambda e, n=n: e.dma_start(
            garep[:],
            modscr[6 * n + 2:6 * n + 3, :].to_broadcast([128, HID])),
            reads=[f"modscr:{n}:2"], writes=["garep"])

    attn_bc(0)
    for step in range(NT + GT):
        if step < NT:
            a_fe(0, step)
        if step - GT >= 0:
            a_mm(0, step - GT, qrot=True)
        if step < NT and (step + 1) % GT == 0:
            a_rd(0, step // GT)
    ctx.dma("G", "wl2", lambda e: e.dma_start(
        WO[:], dwo.rearrange("(k p) f -> p k f", p=128)),
        writes=[f"WO:{k}" for k in range(NCHUNK)], batch=True)
    for g in range(NCHUNK):
        ctx.dma("G", f"wvt{g}", lambda e, g=g: e.dma_start(
            WVTS[:, g, :],
            dwvt.rearrange("(g p) f -> p g f", p=128)[:, g, :]),
            writes=[f"WVTS:{g}", f"wvts_stage:{g}"])
    mid(0)
    for t in range(NT):
        b1_tile(0, t)
    emit_mod([2, 3, 4, 5], hb, ["hb0", "hb1"], sqb, "sqb", lnb[0], "lnb0")
    b1_vtr(0)

    ZIP_MM = True
    bcgarep(0)
    attn_bc(1)
    for rt in range(NT):
        b2_tile(0, rt, psC, "psC", xc, "xc", "xr0")
        a_fe(1, rt)
        if ZIP_MM and rt - GT >= 0:
            a_mm(1, rt - GT, qrot=False)
        if (rt + 1) % GT == 0:
            a_rd(1, rt // GT)
    emit_ffn_batch(0)
    for g in range(NCHUNK):
        ctx.dma("G", f"wvt{g}", lambda e, g=g: e.dma_start(
            WVTS[:, g, :],
            dwvt.rearrange("(g p) f -> p g f", p=128)[:, g, :]),
            writes=[f"WVTS:{g}", f"wvts_stage:{g}"])
    for t in range(0 if not ZIP_MM else NT - GT, NT):
        a_mm(1, t, qrot=(not ZIP_MM))
    mid(1)
    for t in range(NT):
        b1_tile(1, t)
    b1_vtr(1)
    bcgarep(1)
    for rt in range(NT):
        psO, ok = (psB, "psB") if rt % 2 == 0 else (psA, "psA")
        b2_tile(1, rt, psO, ok, xb[rt % NB2], f"xb{rt % NB2}", f"xr{rt % NB2}")
    emit_ffn_batch(1)

    ctx.barrier()

    ast_.close()
    EXPW = [E(nc.sbuf_tensor(f"expw{i}", [128, NCHUNK, HID], BF16))
            for i in range(NEXP)]
    NH2 = 3
    h2Tt = E(nc.sbuf_tensor("h2Tt", [128, NH2, NCHUNK, GS], BF16))
    h2T = [h2Tt[:, g, :, :] for g in range(NH2)]
    exp_views = [(EXPW[i].ap(), [f"ex:{i}"]) for i in range(NEXP)]

    cb1rep = E(nc.sbuf_tensor("cb1rep", [128, HID], BF16))
    sbrep2 = E(nc.sbuf_tensor("sbrep2", [128, HID], BF16))
    gbrep2 = E(nc.sbuf_tensor("gbrep2", [128, HID], BF16))

    def ffn_bcast(n, which):
        sel = {"fe": ((cb1rep, 4, "cb1rep"), (sbrep2, 3, "sbrep2")),
               "mm": ((gbrep2, 5, "gbrep2"),)}[which]
        for rep, slot, key in sel:
            ctx.dma("G", "bc2" + key, lambda e, rep=rep, slot=slot, n=n:
                    e.dma_start(rep[:], modscr[6 * n + slot:6 * n + slot + 1, :]
                                .to_broadcast([128, HID])),
                    writes=[key])
    ffn_bcast(0, "fe")
    ffn_bcast(0, "mm")
    for e_i in range(NEXP):
        ctx.dma("G", f"exl{e_i}", lambda e, e_i=e_i: e.dma_start(
            EXPW[e_i][:], dexp[e_i].rearrange("(k p) f -> p k f", p=128)),
            writes=[f"ex:{e_i}"])

    accb = [E(nc.sbuf_tensor(f"accb{i}", [128, HID], F32)) for i in range(NB2)]
    x1r = [E(nc.sbuf_tensor(f"x1r{i}", [128, HID], F32)) for i in range(2)]

    NGU = NU // GT

    def ffn_fe(u):
        gg = u // GT
        gbuf = gg % 2
        if True:
            n = u // NT
            bi = u % NB2
            xi = u % 2
            ctx.dma("Y", f"f1{xi}", lambda e, u=u, xi=xi: e.dma_start(
                x1r[xi][:], x1scr[128 * u:128 * u + 128, :]),
                writes=[f"x1r{xi}"])
            vop(lambda e, u=u, xi=xi: e.tensor_scalar(
                lnb[u % 2][:], x1r[xi][:], ffst[:, u, 3:4], ffst[:, u, 2:3],
                ALU.mult, ALU.add),
                reads=[f"x1r{xi}", f"ffb:{u // NT}"],
                writes=[f"lnb{u % 2}"])
            if u == NT:
                ffn_bcast(1, "fe")
            vop(lambda e, u=u: e.tensor_mul(sqb[:], lnb[u % 2][:], cb1rep[:]),
                reads=[f"lnb{u % 2}", "cb1rep"], writes=["sqb"])
            vop(lambda e, bi=bi: e.tensor_add(hb[bi][:], sqb[:], sbrep2[:]),
                reads=["sqb", "sbrep2"], writes=[f"hb{bi}"])
            ctx.dma("Y", f"h2w{bi}", lambda e, u=u, bi=bi: e.dma_start(
                h2scr[128 * u:128 * u + 128, :], hb[bi][:]),
                reads=[f"hb{bi}"], writes=[f"h2s:{u}"])
    def ffn_rd(gg):
        gbuf = gg % NH2
        for k in range(NCHUNK):
            ctx.dma("S", f"h2r{k}", lambda e, k=k, gg=gg, gbuf=gbuf:
                    e.dma_start(
                        h2T[gbuf][:, k, :],
                        h2scr[GS * gg:GS * gg + GS, 128 * k:128 * k + 128],
                        transpose=True),
                    reads=[f"h2s:{uu}" for uu in range(GT * gg, GT * (gg + 1))],
                    writes=[f"h2Tg:{gbuf}:{k}"])

    def ffn_mm(u):
        gg = u // GT
        gbuf = gg % NH2
        if True:
            n = u // NT
            t = u % NT
            bi = u % NB2
            toff = 128 * (u - GT * gg)
            if u == NT:
                ffn_bcast(1, "mm")
            for k in range(NCHUNK):
                wkey = ["psE:g"] if k in (0, NCHUNK - 1) else []
                top(lambda e, k=k, gbuf=gbuf, toff=toff: e.matmul(
                    psE[:, 0:8], h2T[gbuf][:, k, toff:toff + 128],
                    gatew[:, k, :], start=(k == 0), stop=(k == NCHUNK - 1)),
                    reads=[f"h2Tg:{gbuf}:{k}", "gatew"],
                    writes=wkey)
            vop(lambda e: e.tensor_add(gsc[:, 0:8], psE[:, 0:8],
                                       gbrep[:, 0:NEXP]),
                reads=["psE:g", "gbrep"], writes=["gp"])
            vop(lambda e: e.tensor_scalar_max(gsc[:, 0:8], gsc[:, 0:8], GATE_EPS),
                reads=["gp"], writes=["gp"])
            sop(lambda e: e.activation(gsc[:, 0:8], gsc[:, 0:8], AF.Exp),
                reads=["gp"], writes=["gp"])
            vop(lambda e: e.reduce_sum(gsc[:, 17:18], gsc[:, 0:8], axis=AX.X),
                reads=["gp"], writes=["gsum"])
            vop(lambda e: e.reciprocal(gsc[:, 18:19], gsc[:, 17:18]),
                reads=["gsum"], writes=["grcp"])
            vop(lambda e: e.tensor_scalar(gsc[:, 0:8], gsc[:, 0:8],
                                          gsc[:, 18:19], None, ALU.mult),
                reads=["gp", "grcp"], writes=["gp"])
            vop(lambda e: e.tensor_add(gsc[:, 48:56], gsc[:, 0:8],
                                       gbrep[:, NEXP:2 * NEXP]),
                reads=["gp", "gbrep"], writes=["gq"])
            vop(lambda e: e.max(gsc[:, 8:16], gsc[:, 48:56]),
                reads=["gq"], writes=["gtmp"])
            vop(lambda e: e.tensor_scalar(gsc[:, 24:32], gsc[:, 48:56],
                                          gsc[:, 9:10], None, ALU.is_ge),
                reads=["gq", "gtmp"], writes=["gmask"])
            vop(lambda e: e.tensor_scalar_max(gsc[:, 32:40], gsc[:, 0:8],
                                              W_THRES),
                reads=["gp"], writes=["gw"])
            vop(lambda e: e.tensor_mul(gsc[:, 32:40], gsc[:, 32:40],
                                       gsc[:, 24:32]),
                reads=["gw", "gmask"], writes=["gw"])
            vop(lambda e: e.reduce_sum(gsc[:, 18:19], gsc[:, 32:40], axis=AX.X),
                reads=["gw"], writes=["gtot"])
            vop(lambda e: e.reciprocal(gsc[:, 18:19], gsc[:, 18:19]),
                reads=["gtot"], writes=["gtot"])
            vop(lambda e: e.tensor_scalar(gsc[:, 40:48], gsc[:, 32:40],
                                          gsc[:, 18:19], 0.5, ALU.mult,
                                          ALU.mult),
                reads=["gw", "gtot"], writes=["comb"])
            acc = accb[bi]
            pss = [(psA, "psA"), (psB, "psB"), (psC, "psC")]
            for e_i in range(NEXP):
                ps, pk = pss[e_i % 3]
                view, keys = exp_views[e_i]
                for k in range(NCHUNK):
                    for h in range(2):
                        wkey = [f"{pk}:{h}"] if k == 0 else []
                        top(lambda e, k=k, h=h, ps=ps, view=view, gbuf=gbuf,
                            toff=toff: e.matmul(
                                ps[:, 512 * h:512 * h + 512],
                                h2T[gbuf][:, k, toff:toff + 128],
                                view[:, k, 512 * h:512 * h + 512],
                                start=(k == 0), stop=False),
                            reads=[f"h2Tg:{gbuf}:{k}"] + keys,
                            writes=wkey)
                bp = ballp[e_i // 4]
                po = 32 * (e_i % 4)
                for h in range(2):
                    top(lambda e, h=h, ps=ps, bp=bp, po=po: e.matmul(
                        ps[:, 512 * h:512 * h + 512],
                        onest[po:po + 1, :],
                        bp[po:po + 1, 512 * h:512 * h + 512],
                        start=False, stop=True,
                        tile_position=(po, 0)),
                        reads=["onest", f"ballp{e_i // 4}"],
                        writes=[f"{pk}:{h}"])
                if e_i == 0:
                    vop(lambda e, ps=ps, acc=acc: e.tensor_scalar(
                        acc[:], ps[:], gsc[:, 40:41], None, ALU.mult),
                        reads=["psA:0", "psA:1", "comb"], writes=[f"accb{bi}"])
                else:
                    vop(lambda e, ps=ps, e_i=e_i, acc=acc, pk=pk:
                        e.scalar_tensor_tensor(
                            acc[:], ps[:], gsc[:, 40 + e_i:41 + e_i], acc[:],
                            ALU.mult, ALU.add),
                        reads=[f"{pk}:0", f"{pk}:1", "comb", f"accb{bi}"],
                        writes=[f"accb{bi}"])
            vop(lambda e, acc=acc: e.tensor_mul(acc[:], acc[:], gbrep2[:]),
                reads=[f"accb{bi}", "gbrep2"], writes=[f"accb{bi}"])
            ctx.dma("G", f"ow{bi}", lambda e, u=u, n=n, t=t, acc=acc, bi=bi:
                    e.dma_start(dout[n, 128 * t:128 * t + 128, :], acc[:],
                                accum_op=ALU.add),
                    reads=[f"accb{bi}", f"outx:{n}:{t}"], writes=[f"out:{u}"])

    FLAG = 2 * GT
    for step in range(NU + FLAG):
        if step < NU:
            ffn_fe(step)
        if step - FLAG >= 0:
            ffn_mm(step - FLAG)
        if step < NU and (step + 1) % GT == 0:
            ffn_rd(step // GT)

    ctx.finish()
    st.close()
    return nc


def _stage_shared(Wq, Wk, Wv, Wo, bo, gate_W, gate_b, expert_W, expert_b, wmod):
    f32 = np.float32
    wq_p = np.ascontiguousarray(
        Wq.reshape(HID, HEADS, HD).transpose(0, 2, 1).reshape(HID, HID),
        dtype=f32)
    wk_p = np.ascontiguousarray(
        Wk.reshape(HID, HEADS, HD).transpose(0, 2, 1).reshape(HID, HID),
        dtype=f32)
    wvt_p = np.ascontiguousarray(
        Wv.T.reshape(HEADS, 8, 8, HID).transpose(1, 2, 0, 3).reshape(HID, HID),
        dtype=f32)
    return {
        "wq": wq_p, "wk": wk_p, "wvt": wvt_p,
        "wo": np.ascontiguousarray(Wo, dtype=f32),
        "bo": np.ascontiguousarray(np.asarray(bo).reshape(1, HID), dtype=f32),
        "gw": np.ascontiguousarray(gate_W, dtype=f32),
        "gbr": np.ascontiguousarray(
            np.broadcast_to(
                np.concatenate([
                    np.asarray(gate_b, dtype=np.float64).reshape(1, NEXP),
                    (np.arange(NEXP - 1, -1, -1, dtype=np.float64)
                     .reshape(1, NEXP) * 1e-16)], axis=1),
                (128, 2 * NEXP)),
            dtype=f32),
        "expw": np.ascontiguousarray(expert_W, dtype=f32),
        "ball": np.ascontiguousarray(expert_b, dtype=f32),
        "wmod": np.ascontiguousarray(wmod, dtype=f32),
        "ident": np.eye(128, dtype=f32),
    }


def kernel(L0_attn_x, L0_attn_Wq, L0_attn_Wk, L0_attn_Wv, L0_attn_Wo,
           L0_attn_bo, L0_ffn_gate_W, L0_ffn_gate_b, L0_ffn_expert_W,
           L0_ffn_expert_b, L0_c, L0_wmod):
    x = np.asarray(L0_attn_x)
    B, S, _ = x.shape
    shared = _stage_shared(
        np.asarray(L0_attn_Wq), np.asarray(L0_attn_Wk), np.asarray(L0_attn_Wv),
        np.asarray(L0_attn_Wo), np.asarray(L0_attn_bo),
        np.asarray(L0_ffn_gate_W), np.asarray(L0_ffn_gate_b),
        np.asarray(L0_ffn_expert_W), np.asarray(L0_ffn_expert_b),
        np.asarray(L0_wmod))
    cT = np.ascontiguousarray(np.asarray(L0_c).T, dtype=np.float32)

    nc = build_program(S)
    in_maps = []
    n_cores = B // N_SEQ
    for c in range(n_cores):
        m = dict(shared)
        m["x"] = np.ascontiguousarray(x[N_SEQ * c:N_SEQ * (c + 1)],
                                      dtype=np.float32)
        m["ct"] = np.ascontiguousarray(cT[:, N_SEQ * c:N_SEQ * (c + 1)])
        in_maps.append(m)
    res = run_bass_kernel_spmd(nc, in_maps, core_ids=list(range(n_cores)))
    out = np.concatenate([r["out"] for r in res.results], axis=0)
    return out.astype(np.float32)


# version 32
# speedup vs baseline: 1.0139x; 1.0139x over previous
import contextlib
import numpy as np
import concourse.bass as bass
import concourse.mybir as mybir
from concourse.bass_utils import run_bass_kernel_spmd

F32 = mybir.dt.float32
BF16 = mybir.dt.bfloat16
AF = mybir.ActivationFunctionType
ALU = mybir.AluOpType
AX = mybir.AxisListType

HID = 1024
HEADS = 16
HD = 64
SPLIT_TR = ()
NCHUNK = 8
N_SEQ = 2
EPS_LN = 1e-5
GATE_EPS = 1e-4
W_THRES = 0.1
NEXP = 8


class Ctx:
    ENGMAP = {"T": "tensor", "V": "vector", "S": "scalar", "G": "gpsimd", "Y": "sync"}

    def __init__(self, nc, block, stack):
        self.nc = nc
        self.block = block
        self.stack = stack
        self.sems = {}
        self.counts = {}
        self.lastw = {}
        self.readers = {}
        self.water = {}
        self.batch = set()
        self.upgrades = {}
        self.pend_eng = None
        self.pend = []
        for e in self.ENGMAP:
            self._mksem("e:" + e)

    def _mksem(self, name):
        h = self.stack.enter_context(self.nc.semaphore(name.replace(":", "_")))
        self.sems[name] = h
        self.counts[name] = 0

    def dsem(self, name, batch=False):
        key = "d:" + name
        if key not in self.sems:
            self._mksem(key)
        if batch:
            self.batch.add(key)
        return key

    def _flush(self):
        if self.pend:
            fns = self.pend
            self.pend = []
            getattr(self.block, self.ENGMAP[self.pend_eng])(
                lambda e, fns=fns: [f(e) for f in fns])

    def _emit(self, eng, fn):
        if self.pend_eng != eng:
            self._flush()
            self.pend_eng = eng
        self.pend.append(fn)

    def _wait(self, eng, dep):
        s, v = dep
        if s.startswith("d:"):
            if s in self.batch:
                v = self.counts[s]
            elif v != self.counts[s]:
                self.upgrades[s] = self.upgrades.get(s, 0) + 1
                v = self.counts[s]
        if v <= self.water.get((eng, s), 0):
            return
        self.water[(eng, s)] = v
        self._emit(eng, lambda e, s=s, v=v: e.wait_ge(self.sems[s], v))

    def _deps(self, reads, writes):
        deps = {}

        def add(t):
            if t is None:
                return
            if isinstance(t, list):
                for tt in t:
                    add(tt)
                return
            deps[t[0]] = max(deps.get(t[0], 0), t[1])

        for k in reads:
            add(self.lastw.get(k))
        for k in writes:
            for t in self.readers.get(k, ()):
                add(t)
            add(self.lastw.get(k))
        return sorted(deps.items())

    def fence_keys(self, keys, sems):
        toks = [(s, self.counts[s]) for s in sems if self.counts.get(s, 0) > 0]
        for k in keys:
            self.lastw[k] = list(toks)

    def _book(self, tok, reads, writes):
        for k in writes:
            self.lastw[k] = tok
            self.readers[k] = []
        for k in reads:
            self.readers.setdefault(k, []).append(tok)
        return tok

    def op(self, eng, fn, reads=(), writes=()):
        for d in self._deps(reads, writes):
            self._wait(eng, d)
        sem = "e:" + eng
        self.counts[sem] += 1
        v = self.counts[sem]
        self._emit(eng, lambda e, fn=fn, sem=sem, v=v:
                   fn(e).then_inc(self.sems[sem], 1))
        return self._book((sem, v), reads, writes)

    def dma(self, eng, dsem_name, fn, reads=(), writes=(), batch=False):
        sem = self.dsem(dsem_name, batch=batch)
        for d in self._deps(reads, writes):
            self._wait(eng, d)
        self.counts[sem] += 16
        v = self.counts[sem]
        self._emit(eng, lambda e, fn=fn, sem=sem, v=v:
                   fn(e).then_inc(self.sems[sem], 16))
        return self._book((sem, v), reads, writes)

    def wait_all_dma(self, eng):
        for s, v in list(self.counts.items()):
            if s.startswith("d:") and v > 0:
                self._wait(eng, (s, v))

    def barrier(self):
        for eng in self.ENGMAP:
            for s, v in list(self.counts.items()):
                if v > 0 and s != "e:" + eng:
                    self._wait(eng, (s, v))
        self.lastw.clear()
        self.readers.clear()

    def finish(self):
        self.wait_all_dma("G")
        self.wait_all_dma("S")
        self.wait_all_dma("Y")
        self._flush()


def build_program(seq: int):
    NT = seq // 128
    T16 = seq // 16
    HPR = 128 // T16 if T16 <= 128 else 0
    assert T16 <= 128 and 128 % T16 == 0
    GT = min(4, NT) if seq >= 2048 else min(2, NT)
    assert NT % GT == 0
    NG = NT // GT
    GS = 128 * GT
    NB2 = 2 if seq >= 2048 else 1
    NU = N_SEQ * NT

    nc = bass.Bass("TRN2", target_bir_lowering=False, debug=False,
                   dynamic_dma_scratch_size=16384)

    dx = nc.dram_tensor("x", [N_SEQ, seq, HID], F32, kind="ExternalInput").ap()
    dwq = nc.dram_tensor("wq", [HID, HID], F32, kind="ExternalInput").ap()
    dwk = nc.dram_tensor("wk", [HID, HID], F32, kind="ExternalInput").ap()
    dwvt = nc.dram_tensor("wvt", [HID, HID], F32, kind="ExternalInput").ap()
    dwo = nc.dram_tensor("wo", [HID, HID], F32, kind="ExternalInput").ap()
    dbo = nc.dram_tensor("bo", [1, HID], F32, kind="ExternalInput").ap()
    dgw = nc.dram_tensor("gw", [HID, NEXP], F32, kind="ExternalInput").ap()
    dgbr = nc.dram_tensor("gbr", [128, 2 * NEXP], F32, kind="ExternalInput").ap()
    dexp = nc.dram_tensor("expw", [NEXP, HID, HID], F32, kind="ExternalInput").ap()
    dball = nc.dram_tensor("ball", [NEXP, HID], F32, kind="ExternalInput").ap()
    dct = nc.dram_tensor("ct", [HID, N_SEQ], F32, kind="ExternalInput").ap()
    dwmod = nc.dram_tensor("wmod", [HID, 6 * HID], F32, kind="ExternalInput").ap()
    did = nc.dram_tensor("ident", [128, 128], F32, kind="ExternalInput").ap()
    attscr = nc.dram_tensor("attscr", [16, 16], BF16).ap()
    dout = nc.dram_tensor("out", [N_SEQ, seq, HID], F32, kind="ExternalOutput").ap()

    h1scr = nc.dram_tensor("h1scr", [N_SEQ, seq, HID], BF16).ap()
    vscr = nc.dram_tensor("vscr", [seq, HID], BF16).ap()
    wvscr = nc.dram_tensor("wvscr", [HID, HID], BF16).ap()
    x1scr = nc.dram_tensor("x1scr", [N_SEQ * seq, HID], F32).ap()
    h2scr = nc.dram_tensor("h2scr", [N_SEQ * seq, HID], BF16).ap()
    modscr = nc.dram_tensor("modscr", [16, HID], F32).ap()
    expscr = nc.dram_tensor("expscr", [NEXP, HID, HID], BF16).ap()

    st = contextlib.ExitStack()
    E = st.enter_context

    psA = E(nc.psum_tensor("psA", [128, 1024], F32))
    psB = E(nc.psum_tensor("psB", [128, 1024], F32))
    psC = E(nc.psum_tensor("psC", [128, 1024], F32))
    psE = E(nc.psum_tensor("psE", [128, 128], F32))
    psX = E(nc.psum_tensor("psX", [128, 256], BF16))




    ones1 = E(nc.sbuf_tensor("ones1", [1, 128], BF16))
    gbrep = E(nc.sbuf_tensor("gbrep", [128, 2 * NEXP], F32))
    ballp = [E(nc.sbuf_tensor(f"ballp{i}", [128, HID], BF16)) for i in range(2)]
    onest = E(nc.sbuf_tensor("onest", [128, 128], BF16))
    gatew = E(nc.sbuf_tensor("gatew", [128, NCHUNK, NEXP], BF16))
    borow_bf = E(nc.sbuf_tensor("borow_bf", [1, HID], BF16))
    ctsb = E(nc.sbuf_tensor("ctsb", [128, NCHUNK, N_SEQ], F32))
    csil = E(nc.sbuf_tensor("csil", [128, NCHUNK, N_SEQ], BF16))

    gsc = E(nc.sbuf_tensor("gsc", [128, 56], F32))
    ffst = E(nc.sbuf_tensor("ffst", [128, NU, 4], F32))
    lnst = E(nc.sbuf_tensor("lnst", [128, NT, 4], F32))

    att_bf = E(nc.sbuf_tensor("att_bf", [16, 16], BF16))


    ctx = Ctx(nc, E(nc.Block()), st)

    def vop(fn, reads=(), writes=()):
        return ctx.op("V", fn, reads, writes)

    def sop(fn, reads=(), writes=()):
        return ctx.op("S", fn, reads, writes)

    def top(fn, reads=(), writes=()):
        return ctx.op("T", fn, reads, writes)

    xb = [E(nc.sbuf_tensor(f"xb{i}", [128, HID], F32)) for i in range(2)]
    sqb = E(nc.sbuf_tensor("sqb", [128, HID], F32))
    sqdump = E(nc.sbuf_tensor("sqdump", [128, HID], BF16))
    lnb = [E(nc.sbuf_tensor(f"lnb{i}", [128, HID], F32)) for i in range(2)]
    hb = [E(nc.sbuf_tensor(f"hb{i}", [128, HID], BF16)) for i in range(NB2)]

    ast_ = contextlib.ExitStack()
    A = ast_.enter_context
    pst = contextlib.ExitStack()
    P = pst.enter_context
    WVP = A(nc.sbuf_tensor("WVP", [128, NCHUNK, HID], BF16, side="right"))
    id128 = A(nc.sbuf_tensor("id128", [128, 128], BF16))
    id128f = A(nc.sbuf_tensor("id128f", [128, 128], F32))
    attT_sb = A(nc.sbuf_tensor("attT_sb", [16, 16], BF16))
    e16 = A(nc.sbuf_tensor("e16", [16, 64], F32))
    bd8 = A(nc.sbuf_tensor("bd8", [128, 128], BF16))
    WVTS = A(nc.sbuf_tensor("WVTS", [128, NCHUNK, HID], BF16, side="right"))
    rowbuf = P(nc.sbuf_tensor("rowbuf", [N_SEQ, HID], F32))
    rowtmp = P(nc.sbuf_tensor("rowtmp", [N_SEQ, HID], F32))

    ctx.dma("G", "cst", lambda e: e.dma_start(id128[:], did), writes=["id128"], batch=True)
    ctx.dma("G", "cst", lambda e: e.dma_start(id128f[:], did), writes=["id128f"], batch=True)
    ctx.dma("G", "cst", lambda e: e.dma_start(gbrep[:], dgbr), writes=["gbrep"], batch=True)
    for e_i in range(NEXP):
        ctx.dma("G", "cst", lambda e, e_i=e_i: e.dma_start(
            ballp[e_i // 4][32 * (e_i % 4):32 * (e_i % 4) + 1, :],
            dball[e_i:e_i + 1, :]),
            writes=[f"ballp{e_i // 4}"], batch=True)
    ctx.dma("G", "cst", lambda e: e.dma_start(
        gatew[:], dgw.rearrange("(k p) e -> p k e", p=128)), writes=["gatew"], batch=True)
    ctx.dma("G", "cst", lambda e: e.dma_start(borow_bf[:], dbo),
            writes=["borow_bf"], batch=True)
    ctx.dma("Y", "cstY", lambda e: e.dma_start(
        ctsb[:], dct.rearrange("(k p) n -> p k n", p=128)), writes=["ct"], batch=True)
    vop(lambda e: e.memset(ones1[:], 1.0), writes=["ones1"])
    vop(lambda e: e.memset(onest[:], 1.0), writes=["onest"])

    sop(lambda e: e.activation(csil[:], ctsb[:], AF.Silu),
        reads=["ct"], writes=["csil"])


    def emit_mod(jbs, ldb, ldk, rb, rbk, rt_, rtk, stage=None):
        rot = [(psA, "psA"), (psB, "psB"), (psC, "psC")]
        for ji, jb in enumerate(jbs):
            ps, pk = rot[ji % 3]
            if stage is not None:
                sbuf3, skey = stage[ji]
                ctx.dma("G", f"wms{ji}", lambda e, jb=jb, sbuf3=sbuf3: e.dma_start(
                    sbuf3[:], dwmod[:, HID * jb:HID * (jb + 1)]
                    .rearrange("(k p) f -> p k f", p=128)),
                    writes=[skey] + [f"{skey}:{k}" for k in range(NCHUNK)])
                for k in range(NCHUNK):
                    for h in range(2):
                        wkey = [f"{pk}:{h}"] if k in (0, NCHUNK - 1) else []
                        top(lambda e, k=k, h=h, ps=ps, sbuf3=sbuf3: e.matmul(
                            ps[0:N_SEQ, 512 * h:512 * h + 512], csil[:, k, :],
                            sbuf3[:, k, 512 * h:512 * h + 512],
                            start=(k == 0), stop=(k == NCHUNK - 1)),
                            reads=["csil", f"{skey}:{k}"], writes=wkey)
            else:
                for k in range(NCHUNK):
                    ctx.dma("G", f"wm{k % 2}", lambda e, k=k, jb=jb: e.dma_start(
                        ldb[k % len(ldb)][:],
                        dwmod[128 * k:128 * k + 128, HID * jb:HID * (jb + 1)]),
                        writes=[ldk[k % len(ldb)]])
                    for h in range(2):
                        wkey = [f"{pk}:{h}"] if k in (0, NCHUNK - 1) else []
                        top(lambda e, k=k, h=h, ps=ps: e.matmul(
                            ps[0:N_SEQ, 512 * h:512 * h + 512], csil[:, k, :],
                            ldb[k % len(ldb)][:, 512 * h:512 * h + 512],
                            start=(k == 0), stop=(k == NCHUNK - 1)),
                            reads=["csil", ldk[k % len(ldb)]], writes=wkey)
            vop(lambda e, ps=ps: e.tensor_copy(rb[0:N_SEQ, :], ps[0:N_SEQ, :]),
                reads=[f"{pk}:0", f"{pk}:1"], writes=[rbk])
            if jb in (1, 4):
                vop(lambda e: e.tensor_scalar_add(rb[0:N_SEQ, :], rb[0:N_SEQ, :], 1.0),
                    reads=[rbk], writes=[rbk])
            elif jb in (2, 5):
                vop(lambda e: e.tensor_scalar(rt_[0:N_SEQ, :], rb[0:N_SEQ, :],
                                              0.0, None, ALU.is_ge),
                    reads=[rbk], writes=[rtk])
                vop(lambda e: e.tensor_scalar(rt_[0:N_SEQ, :], rt_[0:N_SEQ, :],
                                              2.0, -1.0, ALU.mult, ALU.add),
                    reads=[rtk], writes=[rtk])
                vop(lambda e: e.tensor_mul(rb[0:N_SEQ, :], rb[0:N_SEQ, :],
                                           rt_[0:N_SEQ, :]),
                    reads=[rbk, rtk], writes=[rbk])
                vop(lambda e: e.tensor_scalar_max(rb[0:N_SEQ, :],
                                                  rb[0:N_SEQ, :], 0.25),
                    reads=[rbk], writes=[rbk])
                vop(lambda e: e.tensor_mul(rb[0:N_SEQ, :], rb[0:N_SEQ, :],
                                           rt_[0:N_SEQ, :]),
                    reads=[rbk, rtk], writes=[rbk])
            for n in range(N_SEQ):
                ctx.dma("G", "modw", lambda e, n=n, jb=jb: e.dma_start(
                    modscr[6 * n + jb:6 * n + jb + 1, :], rb[n:n + 1, :]),
                    reads=[rbk], writes=[f"modscr:{n}:{jb}"], batch=True)

    emit_mod([0, 1], None, None, rowbuf, "rowbuf",
             rowtmp, "rowtmp", stage=[(WVP, "WVPg"), (WVTS, "wvts_stage")])

    pst.close()
    ctx.fence_keys(["xb0", "xb1", "sqb", "lnb0", "lnb1", "hb0", "hb1"],
                   ["e:T", "e:V", "e:G", "d:wm0", "d:wm1", "d:modw"])

    WQ = A(nc.sbuf_tensor("WQ", [128, NCHUNK, HID], BF16, side="right"))
    WK = A(nc.sbuf_tensor("WK", [128, NCHUNK, HID], BF16, side="right"))
    WO = A(nc.sbuf_tensor("WO", [128, NCHUNK, HID], BF16, side="right"))
    H1T = A(nc.sbuf_tensor("H1T", [128, NCHUNK, seq], BF16, side="right"))
    VPT = A(nc.sbuf_tensor("VPT", [128, NCHUNK, seq], BF16, side="right"))

    for nm, sb, src in (("WQ", WQ, dwq), ("WK", WK, dwk)):
        ctx.dma("G", "wl", lambda e, sb=sb, src=src: e.dma_start(
            sb[:], src.rearrange("(k p) f -> p k f", p=128)),
            writes=[f"{nm}:{k}" for k in range(NCHUNK)], batch=True)

    qsb = [A(nc.sbuf_tensor(f"qsb{i}", [128, HID], BF16, side="right")) for i in range(1)]
    ksb = [A(nc.sbuf_tensor(f"ksb{i}", [128, HID], BF16, side="right")) for i in range(1)]
    vtm = [A(nc.sbuf_tensor(f"vtm{i}", [128, HID], BF16, side="right")) for i in range(1)]
    scrT = [A(nc.sbuf_tensor(f"scrT{i}", [128, NCHUNK, 128], BF16, side="right")) for i in range(1)]
    x1b = [A(nc.sbuf_tensor(f"x1b{i}", [128, HID], F32, side="right")) for i in range(NB2)]
    esb = A(nc.sbuf_tensor("esb", [128, 128], F32, side="right"))
    ca1rep = A(nc.sbuf_tensor("ca1rep", [128, HID], BF16))
    sarep = A(nc.sbuf_tensor("sarep", [128, HID], BF16))
    garep = A(nc.sbuf_tensor("garep", [128, HID], BF16))

    def emit_ffn_stats(xbuf, xkey, u):
        c = ffst[:, u, :]
        k = f"ffr:{u}"
        vop(lambda e, c=c: e.reduce_sum(c[:, 0:1], xbuf[:], axis=AX.X),
            reads=[xkey], writes=[k])
        vop(lambda e, c=c: e.scalar_tensor_tensor(
            sqdump[:], xbuf[:], 1.0, xbuf[:], ALU.mult, ALU.mult,
            accum_out=c[:, 1:2]),
            reads=[xkey], writes=[k, "sqdump"])

    def emit_rstd_batch(F, rk, k):
        vop(lambda e: e.tensor_scalar_mul(F[:, :, 0:1], F[:, :, 0:1], 1.0 / HID),
            reads=rk, writes=[k + ":mean"])
        vop(lambda e: e.tensor_mul(F[:, :, 2:3], F[:, :, 0:1], F[:, :, 0:1]),
            reads=[k + ":mean"], writes=[k + ":msq"])
        vop(lambda e: e.tensor_scalar(F[:, :, 1:2], F[:, :, 1:2], 1.0 / HID,
                                      EPS_LN, ALU.mult, ALU.add),
            reads=rk, writes=[k + ":v1"])
        vop(lambda e: e.tensor_sub(F[:, :, 1:2], F[:, :, 1:2], F[:, :, 2:3]),
            reads=[k + ":v1", k + ":msq"], writes=[k + ":var"])
        vop(lambda e: e.reciprocal(F[:, :, 3:4], F[:, :, 1:2]),
            reads=[k + ":var"], writes=[k + ":rvar"])
        sop(lambda e: e.activation(F[:, :, 3:4], F[:, :, 3:4], AF.Sqrt),
            reads=[k + ":rvar"], writes=[k + ":rstd"])
        vop(lambda e: e.scalar_tensor_tensor(
            F[:, :, 2:3], F[:, :, 0:1], -1.0, F[:, :, 3:4], ALU.mult, ALU.mult),
            reads=[k + ":mean", k + ":rstd"], writes=[k])

    def emit_ffn_batch(n):
        emit_rstd_batch(ffst[:, n * NT:(n + 1) * NT, :],
                        [f"ffr:{u}" for u in range(n * NT, (n + 1) * NT)],
                        f"ffb:{n}")

    xc = A(nc.sbuf_tensor("xc", [128, HID], F32, side="right"))
    exstage = None

    def attn_bc(n):
        for rep, key, slot in ((ca1rep, "ca1rep", 1), (sarep, "sarep", 0)):
            ctx.dma("G", "bc" + key, lambda e, rep=rep, slot=slot, n=n: e.dma_start(
                rep[:],
                modscr[6 * n + slot:6 * n + slot + 1, :].to_broadcast([128, HID])),
                reads=[f"modscr:{n}:{slot}"], writes=[key])

    def a_fe_pre(n, t):
        bi = t % NB2
        ctx.dma("Y", f"xs{bi}", lambda e, t=t, bi=bi, n=n: e.dma_start(
            xb[bi][:], dx[n, 128 * t:128 * t + 128, :]),
            writes=[f"xb{bi}"])
        c = lnst[:, t, :]
        vop(lambda e, c=c, bi=bi: e.reduce_sum(c[:, 0:1], xb[bi][:], axis=AX.X),
            reads=[f"xb{bi}"], writes=[f"lnr:{t}"])
        vop(lambda e, c=c, bi=bi: e.scalar_tensor_tensor(
            sqdump[:], xb[bi][:], 1.0, xb[bi][:], ALU.mult, ALU.mult,
            accum_out=c[:, 1:2]),
            reads=[f"xb{bi}"], writes=[f"lnr:{t}", "sqdump"])

    def a_fe_post(n, t, t0):
        bi = t % NB2
        vop(lambda e, t=t, bi=bi: e.tensor_scalar(
            lnb[t % 2][:], xb[bi][:], lnst[:, t, 3:4], lnst[:, t, 2:3],
            ALU.mult, ALU.add),
            reads=[f"xb{bi}", f"lnp:{t0}"], writes=[f"lnb{t % 2}"])
        vop(lambda e, t=t: e.tensor_mul(sqb[:], lnb[t % 2][:], ca1rep[:]),
            reads=[f"lnb{t % 2}", "ca1rep"], writes=["sqb"])
        vop(lambda e, bi=bi: e.tensor_add(hb[bi][:], sqb[:], sarep[:]),
            reads=["sqb", "sarep"], writes=[f"hb{bi}"])
        ctx.dma("Y", f"h1w{bi}", lambda e, t=t, bi=bi, n=n: e.dma_start(
            h1scr[n, 128 * t:128 * t + 128, :], hb[bi][:]),
            reads=[f"hb{bi}"], writes=[f"h1s:{n}:{t}"])

    PB = NB2

    def a_fe(n, t):
        a_fe_pre(n, t)
        if (t + 1) % PB == 0:
            t0 = t - PB + 1
            emit_rstd_batch(lnst[:, t0:t0 + PB, :],
                            [f"lnr:{tt}" for tt in range(t0, t + 1)],
                            f"lnp:{t0}")
            for tt in range(t0, t + 1):
                a_fe_post(n, tt, t0)

    def a_rd(n, gg):
        for k in range(NCHUNK):
            ctx.dma("S", f"h1r{k}",
                    lambda e, k=k, gg=gg, n=n: e.dma_start(
                        H1T[:, k, GS * gg:GS * gg + GS],
                        h1scr[n, GS * gg:GS * gg + GS, 128 * k:128 * k + 128],
                        transpose=True),
                    reads=[f"h1s:{n}:{tt}"
                           for tt in range(GT * gg, GT * (gg + 1))],
                    writes=[f"h1Tg:{gg}:{k}"])

    def a_mm(n, t, qrot):
        gg = t // GT
        psQ, qk = ((psA, "psA") if (t % 2 == 0 or not qrot) else (psC, "psC"))
        for k in range(NCHUNK):
            for h in range(2):
                wb = [f"psB:{h}"] if k in (0, NCHUNK - 1) else []
                top(lambda e, k=k, h=h, t=t: e.matmul(
                    psB[:, 512 * h:512 * h + 512],
                    H1T[:, k, 128 * t:128 * t + 128],
                    WK[:, k, 512 * h:512 * h + 512],
                    start=(k == 0), stop=(k == NCHUNK - 1)),
                    reads=[f"h1Tg:{gg}:{k}", f"WK:{k}"], writes=wb)
        for k in range(NCHUNK):
            for h in range(2):
                wa = [f"{qk}:{h}"] if k in (0, NCHUNK - 1) else []
                top(lambda e, k=k, h=h, t=t, psQ=psQ: e.matmul(
                    psQ[:, 512 * h:512 * h + 512],
                    H1T[:, k, 128 * t:128 * t + 128],
                    WQ[:, k, 512 * h:512 * h + 512],
                    start=(k == 0), stop=(k == NCHUNK - 1)),
                    reads=[f"h1Tg:{gg}:{k}", f"WQ:{k}"], writes=wa)
        vop(lambda e: e.tensor_copy(ksb[0][:], psB[:]),
            reads=["psB:0", "psB:1"], writes=["ksb0"])
        vop(lambda e, psQ=psQ: e.tensor_copy(qsb[0][:], psQ[:]),
            reads=[f"{qk}:0", f"{qk}:1"], writes=["qsb0"])
        for blk in range(8):
            first = (t == 0 and blk == 0)
            last = (t == NT - 1 and blk == 7)
            top(lambda e, blk=blk, first=first, last=last: e.matmul(
                psE[:, :], qsb[0][:, 128 * blk:128 * blk + 128],
                ksb[0][:, 128 * blk:128 * blk + 128],
                start=first, stop=last),
                reads=["qsb0", "ksb0"],
                writes=(["psEacc"] if (first or last) else []))

    def mid(n):
        vop(lambda e: e.tensor_copy(esb[:], psE[:]),
            reads=["psEacc"], writes=["esb"])
        for dl in range(8):
            pk = "psC:0" if dl < 4 else "psC:1"
            top(lambda e, dl=dl: e.matmul(
                psC[0:16, 128 * dl:128 * dl + 128],
                id128f[:, 16 * dl:16 * dl + 16], esb[:],
                start=True, stop=True),
                reads=["esb", "id128f"], writes=[pk])
        vop(lambda e: e.tensor_copy(e16[0:16, 0:16], psC[0:16, 0:16]),
            reads=["psC:0"], writes=["e16"])
        for dl in range(1, 8):
            off = 128 * dl + 16 * dl
            vop(lambda e, off=off: e.tensor_add(
                e16[0:16, 0:16], e16[0:16, 0:16], psC[0:16, off:off + 16]),
                reads=["psC:0", "psC:1", "e16"], writes=["e16"])
        vop(lambda e: e.tensor_scalar_mul(e16[0:16, 0:16], e16[0:16, 0:16], 0.125),
            reads=["e16"], writes=["e16"])
        vop(lambda e: e.reduce_max(e16[0:16, 16:17], e16[0:16, 0:16], axis=AX.X,
                                   negate=True),
            reads=["e16"], writes=["e16nm"])
        vop(lambda e: e.tensor_scalar(e16[0:16, 32:48], e16[0:16, 0:16],
                                      e16[0:16, 16:17], None, ALU.add),
            reads=["e16", "e16nm"], writes=["e16s"])
        sop(lambda e: e.activation(e16[0:16, 32:48], e16[0:16, 32:48], AF.Exp),
            reads=["e16s"], writes=["e16s"])
        vop(lambda e: e.reduce_sum(e16[0:16, 17:18], e16[0:16, 32:48], axis=AX.X),
            reads=["e16s"], writes=["e16sum"])
        vop(lambda e: e.reciprocal(e16[0:16, 18:19], e16[0:16, 17:18]),
            reads=["e16sum"], writes=["e16rcp"])
        vop(lambda e: e.tensor_scalar(att_bf[:, :], e16[0:16, 32:48],
                                      e16[0:16, 18:19], None, ALU.mult),
            reads=["e16s", "e16rcp"], writes=["att_bf"])
        top(lambda e: e.transpose(psX[0:16, 0:16], att_bf[:, :],
                                  id128[0:16, 0:16]),
            reads=["att_bf", "id128"], writes=["psX:a"])
        vop(lambda e: e.tensor_copy(attT_sb[:], psX[0:16, 0:16]),
            reads=["psX:a"], writes=["attT_sb"])
        ctx.dma("G", "bdw", lambda e: e.dma_start(attscr, attT_sb[:]),
                reads=["attT_sb"], writes=["attscr"], batch=True)
        vop(lambda e: e.memset(bd8[:], 0.0), writes=["bd8"])
        for dl in range(8):
            ctx.dma("G", "bdr", lambda e, dl=dl: e.dma_start(
                bd8[16 * dl:16 * dl + 16, 16 * dl:16 * dl + 16], attscr),
                reads=["attscr"], writes=["bd8"], batch=True)
        for gph in range(NCHUNK):
            for h in range(2):
                top(lambda e, gph=gph, h=h: e.matmul(
                    psC[:, 512 * h:512 * h + 512], bd8[:],
                    WVTS[:, gph, 512 * h:512 * h + 512], start=True, stop=True),
                    reads=["bd8", f"WVTS:{gph}"], writes=[f"psC:{h}"])
            vop(lambda e, gph=gph: e.tensor_copy(WVTS[:, gph, :], psC[:]),
                reads=["psC:0", "psC:1"], writes=[f"WVTS:{gph}"])
            ctx.dma("G", "wvw", lambda e, gph=gph: e.dma_start(
                wvscr.rearrange("(a gg dl) f -> gg dl a f", a=16, gg=8)[gph],
                WVTS[:, gph, :]),
                reads=[f"WVTS:{gph}"], writes=[f"wvscr:{gph}"])
        for k in range(NCHUNK):
            ctx.dma("S", f"wvr{k}", lambda e, k=k: e.dma_start(
                WVP[:, k, :], wvscr[:, 128 * k:128 * k + 128], transpose=True),
                reads=[f"wvscr:{g}" for g in range(NCHUNK)],
                writes=[f"WVPg:{k}"])

    def b1_tile(n, t):
        bi = t % NB2
        gg = t // GT
        psV, vk = (psA, "psA") if t % 2 == 0 else (psB, "psB")
        for k in range(NCHUNK):
            for h in range(2):
                wkey = [f"{vk}:{h}"] if k in (0, NCHUNK - 1) else []
                top(lambda e, k=k, h=h, t=t, psV=psV: e.matmul(
                    psV[:, 512 * h:512 * h + 512],
                    H1T[:, k, 128 * t:128 * t + 128],
                    WVP[:, k, 512 * h:512 * h + 512],
                    start=(k == 0), stop=(k == NCHUNK - 1)),
                    reads=[f"h1Tg:{gg}:{k}", f"WVPg:{k}"],
                    writes=wkey)
        vop(lambda e, psV=psV: e.tensor_copy(vtm[0][:], psV[:]),
            reads=[f"{vk}:0", f"{vk}:1"], writes=["vtm0"])
        ctx.dma("Y", f"vw{t % 2}", lambda e, t=t: e.dma_start(
            vscr.rearrange("(j tg) f -> tg j f", j=16)[8 * t:8 * t + 8, :, :],
            vtm[0][:]),
            reads=["vtm0"], writes=[f"vscr:{t}"])

    def b1_vtr(n):
        for c in range(NCHUNK):
            ctx.dma("S", f"vtr{c}", lambda e, c=c: e.dma_start(
                VPT[:, c, :], vscr[:, 128 * c:128 * c + 128], transpose=True),
                reads=[f"vscr:{t}" for t in range(NT)],
                writes=[f"VPTg:{c}"])

    def b2_tile(n, rt, psO, ok, xbuf, xk, xsem):
        bi = rt % NB2
        for al in range(HPR):
            a = rt * HPR + al
            c = a // 2
            po = (a % 2) * 64
            srcv = VPT[po:po + 64, c, :].rearrange(
                "p (mj jl tg) -> p mj jl tg", mj=8, jl=2)
            for jl in range(2):
                vop(lambda e, srcv=srcv, jl=jl, al=al: e.tensor_copy(
                    scrT[0][64 * jl:64 * jl + 64, :,
                            T16 * al:T16 * al + T16],
                    srcv[:, :, jl, :]),
                    reads=[f"VPTg:{a // 2}"], writes=["scrT0"])
        for m in range(NCHUNK):
            for h in range(2):
                wkey = [f"{ok}:{h}"] if m == 0 else []
                top(lambda e, m=m, h=h, psO=psO: e.matmul(
                    psO[:, 512 * h:512 * h + 512], scrT[0][:, m, :],
                    WO[:, m, 512 * h:512 * h + 512],
                    start=(m == 0), stop=False),
                    reads=["scrT0", f"WO:{m}"], writes=wkey)
        for h in range(2):
            top(lambda e, h=h, psO=psO: e.matmul(
                psO[:, 512 * h:512 * h + 512], ones1[:],
                borow_bf[:, 512 * h:512 * h + 512], start=False, stop=True),
                reads=["ones1", "borow_bf"], writes=[f"{ok}:{h}"])
        ctx.dma("Y", xsem, lambda e, rt=rt, n=n, xbuf=xbuf: e.dma_start(
            xbuf[:], dx[n, 128 * rt:128 * rt + 128, :]),
            writes=[xk])
        vop(lambda e, bi=bi, psO=psO: e.tensor_mul(x1b[bi][:], psO[:],
                                                   garep[:]),
            reads=[f"{ok}:0", f"{ok}:1", "garep"], writes=[f"x1b{bi}"])
        vop(lambda e, bi=bi, xbuf=xbuf: e.tensor_add(x1b[bi][:], x1b[bi][:],
                                                     xbuf[:]),
            reads=[f"x1b{bi}", xk], writes=[f"x1b{bi}"])
        emit_ffn_stats(x1b[bi], f"x1b{bi}", n * NT + rt)
        ctx.dma("Y", f"x1w{bi}", lambda e, rt=rt, bi=bi, n=n: e.dma_start(
            x1scr[seq * n + 128 * rt: seq * n + 128 * rt + 128, :],
            x1b[bi][:]),
            reads=[f"x1b{bi}"], writes=[f"x1s:{n}:{rt}"])
        ctx.dma("Y", f"xow{bi}", lambda e, rt=rt, bi=bi, n=n: e.dma_start(
            dout[n, 128 * rt:128 * rt + 128, :], x1b[bi][:]),
            reads=[f"x1b{bi}"], writes=[f"outx:{n}:{rt}"])

    def bcgarep(n):
        ctx.dma("G", "bcgarep", lambda e, n=n: e.dma_start(
            garep[:],
            modscr[6 * n + 2:6 * n + 3, :].to_broadcast([128, HID])),
            reads=[f"modscr:{n}:2"], writes=["garep"])

    attn_bc(0)
    for step in range(NT + GT):
        if step < NT:
            a_fe(0, step)
        if step - GT >= 0:
            a_mm(0, step - GT, qrot=True)
        if step < NT and (step + 1) % GT == 0:
            a_rd(0, step // GT)
    for m in range(NCHUNK):
        ctx.dma("G", f"wl2{m}", lambda e, m=m: e.dma_start(
            WO[:, m, :], dwo.rearrange("(k p) f -> p k f", p=128)[:, m, :]),
            writes=[f"WO:{m}"])
    for g in range(NCHUNK):
        ctx.dma("G", f"wvt{g}", lambda e, g=g: e.dma_start(
            WVTS[:, g, :],
            dwvt.rearrange("(g p) f -> p g f", p=128)[:, g, :]),
            writes=[f"WVTS:{g}", f"wvts_stage:{g}"])
    mid(0)
    for t in range(NT):
        b1_tile(0, t)
    emit_mod([2, 3, 4, 5], hb, ["hb0", "hb1"], sqb, "sqb", lnb[0], "lnb0")
    b1_vtr(0)

    ZIP_MM = True
    bcgarep(0)
    attn_bc(1)
    for rt in range(NT):
        b2_tile(0, rt, psC, "psC", xc, "xc", "xr0")
        a_fe(1, rt)
        if ZIP_MM and rt - GT >= 0:
            a_mm(1, rt - GT, qrot=False)
        if (rt + 1) % GT == 0:
            a_rd(1, rt // GT)
    emit_ffn_batch(0)
    for g in range(NCHUNK):
        ctx.dma("G", f"wvt{g}", lambda e, g=g: e.dma_start(
            WVTS[:, g, :],
            dwvt.rearrange("(g p) f -> p g f", p=128)[:, g, :]),
            writes=[f"WVTS:{g}", f"wvts_stage:{g}"])
    for t in range(0 if not ZIP_MM else NT - GT, NT):
        a_mm(1, t, qrot=(not ZIP_MM))
    mid(1)
    for t in range(NT):
        b1_tile(1, t)
    b1_vtr(1)
    bcgarep(1)
    for rt in range(NT):
        psO, ok = (psB, "psB") if rt % 2 == 0 else (psA, "psA")
        b2_tile(1, rt, psO, ok, xb[rt % NB2], f"xb{rt % NB2}", f"xr{rt % NB2}")
    emit_ffn_batch(1)

    ctx.barrier()

    ast_.close()
    EXPW = [E(nc.sbuf_tensor(f"expw{i}", [128, NCHUNK, HID], BF16))
            for i in range(NEXP)]
    NH2 = 3
    h2Tt = E(nc.sbuf_tensor("h2Tt", [128, NH2, NCHUNK, GS], BF16))
    h2T = [h2Tt[:, g, :, :] for g in range(NH2)]
    exp_views = [(EXPW[i].ap(), [f"ex:{i}"]) for i in range(NEXP)]

    cb1rep = E(nc.sbuf_tensor("cb1rep", [128, HID], BF16))
    sbrep2 = E(nc.sbuf_tensor("sbrep2", [128, HID], BF16))
    gbrep2 = E(nc.sbuf_tensor("gbrep2", [128, HID], BF16))

    def ffn_bcast(n, which):
        sel = {"fe": ((cb1rep, 4, "cb1rep"), (sbrep2, 3, "sbrep2")),
               "mm": ((gbrep2, 5, "gbrep2"),)}[which]
        for rep, slot, key in sel:
            ctx.dma("G", "bc2" + key, lambda e, rep=rep, slot=slot, n=n:
                    e.dma_start(rep[:], modscr[6 * n + slot:6 * n + slot + 1, :]
                                .to_broadcast([128, HID])),
                    writes=[key])
    ffn_bcast(0, "fe")
    ffn_bcast(0, "mm")
    for e_i in range(NEXP):
        ctx.dma("G", f"exl{e_i}", lambda e, e_i=e_i: e.dma_start(
            EXPW[e_i][:], dexp[e_i].rearrange("(k p) f -> p k f", p=128)),
            writes=[f"ex:{e_i}"])

    accb = [E(nc.sbuf_tensor(f"accb{i}", [128, HID], F32)) for i in range(NB2)]
    x1r = [E(nc.sbuf_tensor(f"x1r{i}", [128, HID], F32)) for i in range(2)]

    NGU = NU // GT

    def ffn_fe(u):
        gg = u // GT
        gbuf = gg % 2
        if True:
            n = u // NT
            bi = u % NB2
            xi = u % 2
            ctx.dma("Y", f"f1{xi}", lambda e, u=u, xi=xi: e.dma_start(
                x1r[xi][:], x1scr[128 * u:128 * u + 128, :]),
                writes=[f"x1r{xi}"])
            vop(lambda e, u=u, xi=xi: e.tensor_scalar(
                lnb[u % 2][:], x1r[xi][:], ffst[:, u, 3:4], ffst[:, u, 2:3],
                ALU.mult, ALU.add),
                reads=[f"x1r{xi}", f"ffb:{u // NT}"],
                writes=[f"lnb{u % 2}"])
            if u == NT:
                ffn_bcast(1, "fe")
            vop(lambda e, u=u: e.tensor_mul(sqb[:], lnb[u % 2][:], cb1rep[:]),
                reads=[f"lnb{u % 2}", "cb1rep"], writes=["sqb"])
            vop(lambda e, bi=bi: e.tensor_add(hb[bi][:], sqb[:], sbrep2[:]),
                reads=["sqb", "sbrep2"], writes=[f"hb{bi}"])
            ctx.dma("Y", f"h2w{bi}", lambda e, u=u, bi=bi: e.dma_start(
                h2scr[128 * u:128 * u + 128, :], hb[bi][:]),
                reads=[f"hb{bi}"], writes=[f"h2s:{u}"])
    def ffn_rd(gg):
        gbuf = gg % NH2
        for k in range(NCHUNK):
            ctx.dma("S", f"h2r{k}", lambda e, k=k, gg=gg, gbuf=gbuf:
                    e.dma_start(
                        h2T[gbuf][:, k, :],
                        h2scr[GS * gg:GS * gg + GS, 128 * k:128 * k + 128],
                        transpose=True),
                    reads=[f"h2s:{uu}" for uu in range(GT * gg, GT * (gg + 1))],
                    writes=[f"h2Tg:{gbuf}:{k}"])

    def ffn_mm(u):
        gg = u // GT
        gbuf = gg % NH2
        if True:
            n = u // NT
            t = u % NT
            bi = u % NB2
            toff = 128 * (u - GT * gg)
            if u == NT:
                ffn_bcast(1, "mm")
            for k in range(NCHUNK):
                wkey = ["psE:g"] if k in (0, NCHUNK - 1) else []
                top(lambda e, k=k, gbuf=gbuf, toff=toff: e.matmul(
                    psE[:, 0:8], h2T[gbuf][:, k, toff:toff + 128],
                    gatew[:, k, :], start=(k == 0), stop=(k == NCHUNK - 1)),
                    reads=[f"h2Tg:{gbuf}:{k}", "gatew"],
                    writes=wkey)
            vop(lambda e: e.tensor_add(gsc[:, 0:8], psE[:, 0:8],
                                       gbrep[:, 0:NEXP]),
                reads=["psE:g", "gbrep"], writes=["gp"])
            vop(lambda e: e.tensor_scalar_max(gsc[:, 0:8], gsc[:, 0:8], GATE_EPS),
                reads=["gp"], writes=["gp"])
            sop(lambda e: e.activation(gsc[:, 0:8], gsc[:, 0:8], AF.Exp),
                reads=["gp"], writes=["gp"])
            vop(lambda e: e.reduce_sum(gsc[:, 17:18], gsc[:, 0:8], axis=AX.X),
                reads=["gp"], writes=["gsum"])
            vop(lambda e: e.reciprocal(gsc[:, 18:19], gsc[:, 17:18]),
                reads=["gsum"], writes=["grcp"])
            vop(lambda e: e.tensor_scalar(gsc[:, 0:8], gsc[:, 0:8],
                                          gsc[:, 18:19], None, ALU.mult),
                reads=["gp", "grcp"], writes=["gp"])
            vop(lambda e: e.tensor_add(gsc[:, 48:56], gsc[:, 0:8],
                                       gbrep[:, NEXP:2 * NEXP]),
                reads=["gp", "gbrep"], writes=["gq"])
            vop(lambda e: e.max(gsc[:, 8:16], gsc[:, 48:56]),
                reads=["gq"], writes=["gtmp"])
            vop(lambda e: e.tensor_scalar(gsc[:, 24:32], gsc[:, 48:56],
                                          gsc[:, 9:10], None, ALU.is_ge),
                reads=["gq", "gtmp"], writes=["gmask"])
            vop(lambda e: e.tensor_scalar_max(gsc[:, 32:40], gsc[:, 0:8],
                                              W_THRES),
                reads=["gp"], writes=["gw"])
            vop(lambda e: e.tensor_mul(gsc[:, 32:40], gsc[:, 32:40],
                                       gsc[:, 24:32]),
                reads=["gw", "gmask"], writes=["gw"])
            vop(lambda e: e.reduce_sum(gsc[:, 18:19], gsc[:, 32:40], axis=AX.X),
                reads=["gw"], writes=["gtot"])
            vop(lambda e: e.reciprocal(gsc[:, 18:19], gsc[:, 18:19]),
                reads=["gtot"], writes=["gtot"])
            vop(lambda e: e.tensor_scalar(gsc[:, 40:48], gsc[:, 32:40],
                                          gsc[:, 18:19], 0.5, ALU.mult,
                                          ALU.mult),
                reads=["gw", "gtot"], writes=["comb"])
            acc = accb[bi]
            pss = [(psA, "psA"), (psB, "psB"), (psC, "psC")]
            for e_i in range(NEXP):
                ps, pk = pss[e_i % 3]
                view, keys = exp_views[e_i]
                for k in range(NCHUNK):
                    for h in range(2):
                        wkey = [f"{pk}:{h}"] if k == 0 else []
                        top(lambda e, k=k, h=h, ps=ps, view=view, gbuf=gbuf,
                            toff=toff: e.matmul(
                                ps[:, 512 * h:512 * h + 512],
                                h2T[gbuf][:, k, toff:toff + 128],
                                view[:, k, 512 * h:512 * h + 512],
                                start=(k == 0), stop=False),
                            reads=[f"h2Tg:{gbuf}:{k}"] + keys,
                            writes=wkey)
                bp = ballp[e_i // 4]
                po = 32 * (e_i % 4)
                for h in range(2):
                    top(lambda e, h=h, ps=ps, bp=bp, po=po: e.matmul(
                        ps[:, 512 * h:512 * h + 512],
                        onest[po:po + 1, :],
                        bp[po:po + 1, 512 * h:512 * h + 512],
                        start=False, stop=True,
                        tile_position=(po, 0)),
                        reads=["onest", f"ballp{e_i // 4}"],
                        writes=[f"{pk}:{h}"])
                if e_i == 0:
                    vop(lambda e, ps=ps, acc=acc: e.tensor_scalar(
                        acc[:], ps[:], gsc[:, 40:41], None, ALU.mult),
                        reads=["psA:0", "psA:1", "comb"], writes=[f"accb{bi}"])
                else:
                    vop(lambda e, ps=ps, e_i=e_i, acc=acc, pk=pk:
                        e.scalar_tensor_tensor(
                            acc[:], ps[:], gsc[:, 40 + e_i:41 + e_i], acc[:],
                            ALU.mult, ALU.add),
                        reads=[f"{pk}:0", f"{pk}:1", "comb", f"accb{bi}"],
                        writes=[f"accb{bi}"])
            vop(lambda e, acc=acc: e.tensor_mul(acc[:], acc[:], gbrep2[:]),
                reads=[f"accb{bi}", "gbrep2"], writes=[f"accb{bi}"])
            ctx.dma("G", f"ow{bi}", lambda e, u=u, n=n, t=t, acc=acc, bi=bi:
                    e.dma_start(dout[n, 128 * t:128 * t + 128, :], acc[:],
                                accum_op=ALU.add),
                    reads=[f"accb{bi}", f"outx:{n}:{t}"], writes=[f"out:{u}"])

    FLAG = 2 * GT
    for step in range(NU + FLAG):
        if step < NU:
            ffn_fe(step)
        if step - FLAG >= 0:
            ffn_mm(step - FLAG)
        if step < NU and (step + 1) % GT == 0:
            ffn_rd(step // GT)

    ctx.finish()
    st.close()
    return nc


def _stage_shared(Wq, Wk, Wv, Wo, bo, gate_W, gate_b, expert_W, expert_b, wmod):
    f32 = np.float32
    wq_p = np.ascontiguousarray(
        Wq.reshape(HID, HEADS, HD).transpose(0, 2, 1).reshape(HID, HID),
        dtype=f32)
    wk_p = np.ascontiguousarray(
        Wk.reshape(HID, HEADS, HD).transpose(0, 2, 1).reshape(HID, HID),
        dtype=f32)
    wvt_p = np.ascontiguousarray(
        Wv.T.reshape(HEADS, 8, 8, HID).transpose(1, 2, 0, 3).reshape(HID, HID),
        dtype=f32)
    return {
        "wq": wq_p, "wk": wk_p, "wvt": wvt_p,
        "wo": np.ascontiguousarray(Wo, dtype=f32),
        "bo": np.ascontiguousarray(np.asarray(bo).reshape(1, HID), dtype=f32),
        "gw": np.ascontiguousarray(gate_W, dtype=f32),
        "gbr": np.ascontiguousarray(
            np.broadcast_to(
                np.concatenate([
                    np.asarray(gate_b, dtype=np.float64).reshape(1, NEXP),
                    (np.arange(NEXP - 1, -1, -1, dtype=np.float64)
                     .reshape(1, NEXP) * 1e-16)], axis=1),
                (128, 2 * NEXP)),
            dtype=f32),
        "expw": np.ascontiguousarray(expert_W, dtype=f32),
        "ball": np.ascontiguousarray(expert_b, dtype=f32),
        "wmod": np.ascontiguousarray(wmod, dtype=f32),
        "ident": np.eye(128, dtype=f32),
    }


def kernel(L0_attn_x, L0_attn_Wq, L0_attn_Wk, L0_attn_Wv, L0_attn_Wo,
           L0_attn_bo, L0_ffn_gate_W, L0_ffn_gate_b, L0_ffn_expert_W,
           L0_ffn_expert_b, L0_c, L0_wmod):
    x = np.asarray(L0_attn_x)
    B, S, _ = x.shape
    shared = _stage_shared(
        np.asarray(L0_attn_Wq), np.asarray(L0_attn_Wk), np.asarray(L0_attn_Wv),
        np.asarray(L0_attn_Wo), np.asarray(L0_attn_bo),
        np.asarray(L0_ffn_gate_W), np.asarray(L0_ffn_gate_b),
        np.asarray(L0_ffn_expert_W), np.asarray(L0_ffn_expert_b),
        np.asarray(L0_wmod))
    cT = np.ascontiguousarray(np.asarray(L0_c).T, dtype=np.float32)

    nc = build_program(S)
    in_maps = []
    n_cores = B // N_SEQ
    for c in range(n_cores):
        m = dict(shared)
        m["x"] = np.ascontiguousarray(x[N_SEQ * c:N_SEQ * (c + 1)],
                                      dtype=np.float32)
        m["ct"] = np.ascontiguousarray(cT[:, N_SEQ * c:N_SEQ * (c + 1)])
        in_maps.append(m)
    res = run_bass_kernel_spmd(nc, in_maps, core_ids=list(range(n_cores)))
    out = np.concatenate([r["out"] for r in res.results], axis=0)
    return out.astype(np.float32)


# version 33
# speedup vs baseline: 1.0191x; 1.0051x over previous
import contextlib
import numpy as np
import concourse.bass as bass
import concourse.mybir as mybir
from concourse.bass_utils import run_bass_kernel_spmd

F32 = mybir.dt.float32
BF16 = mybir.dt.bfloat16
AF = mybir.ActivationFunctionType
ALU = mybir.AluOpType
AX = mybir.AxisListType

HID = 1024
HEADS = 16
HD = 64
SPLIT_TR = ()
NCHUNK = 8
N_SEQ = 2
EPS_LN = 1e-5
GATE_EPS = 1e-4
W_THRES = 0.1
NEXP = 8


class Ctx:
    ENGMAP = {"T": "tensor", "V": "vector", "S": "scalar", "G": "gpsimd", "Y": "sync"}

    def __init__(self, nc, block, stack):
        self.nc = nc
        self.block = block
        self.stack = stack
        self.sems = {}
        self.counts = {}
        self.lastw = {}
        self.readers = {}
        self.water = {}
        self.batch = set()
        self.upgrades = {}
        self.pend_eng = None
        self.pend = []
        for e in self.ENGMAP:
            self._mksem("e:" + e)

    def _mksem(self, name):
        h = self.stack.enter_context(self.nc.semaphore(name.replace(":", "_")))
        self.sems[name] = h
        self.counts[name] = 0

    def dsem(self, name, batch=False):
        key = "d:" + name
        if key not in self.sems:
            self._mksem(key)
        if batch:
            self.batch.add(key)
        return key

    def _flush(self):
        if self.pend:
            fns = self.pend
            self.pend = []
            getattr(self.block, self.ENGMAP[self.pend_eng])(
                lambda e, fns=fns: [f(e) for f in fns])

    def _emit(self, eng, fn):
        if self.pend_eng != eng:
            self._flush()
            self.pend_eng = eng
        self.pend.append(fn)

    def _wait(self, eng, dep):
        s, v = dep
        if s.startswith("d:"):
            if s in self.batch:
                v = self.counts[s]
            elif v != self.counts[s]:
                self.upgrades[s] = self.upgrades.get(s, 0) + 1
                v = self.counts[s]
        if v <= self.water.get((eng, s), 0):
            return
        self.water[(eng, s)] = v
        self._emit(eng, lambda e, s=s, v=v: e.wait_ge(self.sems[s], v))

    def _deps(self, reads, writes):
        deps = {}

        def add(t):
            if t is None:
                return
            if isinstance(t, list):
                for tt in t:
                    add(tt)
                return
            deps[t[0]] = max(deps.get(t[0], 0), t[1])

        for k in reads:
            add(self.lastw.get(k))
        for k in writes:
            for t in self.readers.get(k, ()):
                add(t)
            add(self.lastw.get(k))
        return sorted(deps.items())

    def fence_keys(self, keys, sems):
        toks = [(s, self.counts[s]) for s in sems if self.counts.get(s, 0) > 0]
        for k in keys:
            self.lastw[k] = list(toks)

    def _book(self, tok, reads, writes):
        for k in writes:
            self.lastw[k] = tok
            self.readers[k] = []
        for k in reads:
            self.readers.setdefault(k, []).append(tok)
        return tok

    def op(self, eng, fn, reads=(), writes=()):
        for d in self._deps(reads, writes):
            self._wait(eng, d)
        sem = "e:" + eng
        self.counts[sem] += 1
        v = self.counts[sem]
        self._emit(eng, lambda e, fn=fn, sem=sem, v=v:
                   fn(e).then_inc(self.sems[sem], 1))
        return self._book((sem, v), reads, writes)

    def dma(self, eng, dsem_name, fn, reads=(), writes=(), batch=False):
        sem = self.dsem(dsem_name, batch=batch)
        for d in self._deps(reads, writes):
            self._wait(eng, d)
        self.counts[sem] += 16
        v = self.counts[sem]
        self._emit(eng, lambda e, fn=fn, sem=sem, v=v:
                   fn(e).then_inc(self.sems[sem], 16))
        return self._book((sem, v), reads, writes)

    def wait_all_dma(self, eng):
        for s, v in list(self.counts.items()):
            if s.startswith("d:") and v > 0:
                self._wait(eng, (s, v))

    def barrier(self):
        for eng in self.ENGMAP:
            for s, v in list(self.counts.items()):
                if v > 0 and s != "e:" + eng:
                    self._wait(eng, (s, v))
        self.lastw.clear()
        self.readers.clear()

    def finish(self):
        self.wait_all_dma("G")
        self.wait_all_dma("S")
        self.wait_all_dma("Y")
        self._flush()


def build_program(seq: int):
    NT = seq // 128
    T16 = seq // 16
    HPR = 128 // T16 if T16 <= 128 else 0
    assert T16 <= 128 and 128 % T16 == 0
    GT = min(4, NT) if seq >= 2048 else min(2, NT)
    assert NT % GT == 0
    NG = NT // GT
    GS = 128 * GT
    NB2 = 2 if seq >= 2048 else 1
    NU = N_SEQ * NT

    nc = bass.Bass("TRN2", target_bir_lowering=False, debug=False,
                   dynamic_dma_scratch_size=16384)

    dx = nc.dram_tensor("x", [N_SEQ, seq, HID], F32, kind="ExternalInput").ap()
    dwq = nc.dram_tensor("wq", [HID, HID], F32, kind="ExternalInput").ap()
    dwk = nc.dram_tensor("wk", [HID, HID], F32, kind="ExternalInput").ap()
    dwvt = nc.dram_tensor("wvt", [HID, HID], F32, kind="ExternalInput").ap()
    dwo = nc.dram_tensor("wo", [HID, HID], F32, kind="ExternalInput").ap()
    dbo = nc.dram_tensor("bo", [1, HID], F32, kind="ExternalInput").ap()
    dgw = nc.dram_tensor("gw", [HID, NEXP], F32, kind="ExternalInput").ap()
    dgbr = nc.dram_tensor("gbr", [128, 2 * NEXP], F32, kind="ExternalInput").ap()
    dexp = nc.dram_tensor("expw", [NEXP, HID, HID], F32, kind="ExternalInput").ap()
    dball = nc.dram_tensor("ball", [NEXP, HID], F32, kind="ExternalInput").ap()
    dct = nc.dram_tensor("ct", [HID, N_SEQ], F32, kind="ExternalInput").ap()
    dwmod = nc.dram_tensor("wmod", [HID, 6 * HID], F32, kind="ExternalInput").ap()
    did = nc.dram_tensor("ident", [128, 128], F32, kind="ExternalInput").ap()
    attscr = nc.dram_tensor("attscr", [16, 16], BF16).ap()
    dout = nc.dram_tensor("out", [N_SEQ, seq, HID], F32, kind="ExternalOutput").ap()

    h1scr = nc.dram_tensor("h1scr", [N_SEQ, seq, HID], BF16).ap()
    vscr = nc.dram_tensor("vscr", [seq, HID], BF16).ap()
    wvscr = nc.dram_tensor("wvscr", [HID, HID], BF16).ap()
    x1scr = nc.dram_tensor("x1scr", [N_SEQ * seq, HID], F32).ap()
    h2scr = nc.dram_tensor("h2scr", [N_SEQ * seq, HID], BF16).ap()
    modscr = nc.dram_tensor("modscr", [16, HID], F32).ap()
    expscr = nc.dram_tensor("expscr", [NEXP, HID, HID], BF16).ap()

    st = contextlib.ExitStack()
    E = st.enter_context

    psA = E(nc.psum_tensor("psA", [128, 1024], F32))
    psB = E(nc.psum_tensor("psB", [128, 1024], F32))
    psC = E(nc.psum_tensor("psC", [128, 1024], F32))
    psE = E(nc.psum_tensor("psE", [128, 128], F32))
    psX = E(nc.psum_tensor("psX", [128, 256], BF16))




    ones1 = E(nc.sbuf_tensor("ones1", [1, 128], BF16))
    gbrep = E(nc.sbuf_tensor("gbrep", [128, 2 * NEXP], F32))
    ballp = [E(nc.sbuf_tensor(f"ballp{i}", [128, HID], BF16)) for i in range(2)]
    onest = E(nc.sbuf_tensor("onest", [128, 128], BF16))
    gatew = E(nc.sbuf_tensor("gatew", [128, NCHUNK, NEXP], BF16))
    borow_bf = E(nc.sbuf_tensor("borow_bf", [1, HID], BF16))
    ctsb = E(nc.sbuf_tensor("ctsb", [128, NCHUNK, N_SEQ], F32))
    csil = E(nc.sbuf_tensor("csil", [128, NCHUNK, N_SEQ], BF16))

    gsc = E(nc.sbuf_tensor("gsc", [128, 56], F32))
    ffst = E(nc.sbuf_tensor("ffst", [128, NU, 4], F32))
    lnst = E(nc.sbuf_tensor("lnst", [128, NT, 4], F32))

    att_bf = E(nc.sbuf_tensor("att_bf", [16, 16], BF16))


    ctx = Ctx(nc, E(nc.Block()), st)

    def vop(fn, reads=(), writes=()):
        return ctx.op("V", fn, reads, writes)

    def sop(fn, reads=(), writes=()):
        return ctx.op("S", fn, reads, writes)

    def top(fn, reads=(), writes=()):
        return ctx.op("T", fn, reads, writes)

    xb = [E(nc.sbuf_tensor(f"xb{i}", [128, HID], F32)) for i in range(2)]
    sqb = E(nc.sbuf_tensor("sqb", [128, HID], F32))
    lnb = [E(nc.sbuf_tensor(f"lnb{i}", [128, HID], F32)) for i in range(2)]
    hb = [E(nc.sbuf_tensor(f"hb{i}", [128, HID], BF16)) for i in range(NB2)]

    ast_ = contextlib.ExitStack()
    A = ast_.enter_context
    pst = contextlib.ExitStack()
    P = pst.enter_context
    WVP = A(nc.sbuf_tensor("WVP", [128, NCHUNK, HID], BF16, side="right"))
    id128 = A(nc.sbuf_tensor("id128", [128, 128], BF16))
    id128f = A(nc.sbuf_tensor("id128f", [128, 128], F32))
    attT_sb = A(nc.sbuf_tensor("attT_sb", [16, 16], BF16))
    e16 = A(nc.sbuf_tensor("e16", [16, 64], F32))
    bd8 = A(nc.sbuf_tensor("bd8", [128, 128], BF16))
    WVTS = A(nc.sbuf_tensor("WVTS", [128, NCHUNK, HID], BF16, side="right"))
    rowbuf = P(nc.sbuf_tensor("rowbuf", [N_SEQ, HID], F32))
    rowtmp = P(nc.sbuf_tensor("rowtmp", [N_SEQ, HID], F32))

    ctx.dma("G", "cst", lambda e: e.dma_start(id128[:], did), writes=["id128"], batch=True)
    ctx.dma("G", "cst", lambda e: e.dma_start(id128f[:], did), writes=["id128f"], batch=True)
    ctx.dma("G", "cst", lambda e: e.dma_start(gbrep[:], dgbr), writes=["gbrep"], batch=True)
    for e_i in range(NEXP):
        ctx.dma("G", "cst", lambda e, e_i=e_i: e.dma_start(
            ballp[e_i // 4][32 * (e_i % 4):32 * (e_i % 4) + 1, :],
            dball[e_i:e_i + 1, :]),
            writes=[f"ballp{e_i // 4}"], batch=True)
    ctx.dma("G", "cst", lambda e: e.dma_start(
        gatew[:], dgw.rearrange("(k p) e -> p k e", p=128)), writes=["gatew"], batch=True)
    ctx.dma("G", "cst", lambda e: e.dma_start(borow_bf[:], dbo),
            writes=["borow_bf"], batch=True)
    ctx.dma("Y", "cstY", lambda e: e.dma_start(
        ctsb[:], dct.rearrange("(k p) n -> p k n", p=128)), writes=["ct"], batch=True)
    vop(lambda e: e.memset(ones1[:], 1.0), writes=["ones1"])
    vop(lambda e: e.memset(onest[:], 1.0), writes=["onest"])

    sop(lambda e: e.activation(csil[:], ctsb[:], AF.Silu),
        reads=["ct"], writes=["csil"])


    def emit_mod(jbs, ldb, ldk, rb, rbk, rt_, rtk, stage=None):
        rot = [(psA, "psA"), (psB, "psB"), (psC, "psC")]
        for ji, jb in enumerate(jbs):
            ps, pk = rot[ji % 3]
            if stage is not None:
                sbuf3, skey = stage[ji]
                ctx.dma("G", f"wms{ji}", lambda e, jb=jb, sbuf3=sbuf3: e.dma_start(
                    sbuf3[:], dwmod[:, HID * jb:HID * (jb + 1)]
                    .rearrange("(k p) f -> p k f", p=128)),
                    writes=[skey] + [f"{skey}:{k}" for k in range(NCHUNK)])
                for k in range(NCHUNK):
                    for h in range(2):
                        wkey = [f"{pk}:{h}"] if k in (0, NCHUNK - 1) else []
                        top(lambda e, k=k, h=h, ps=ps, sbuf3=sbuf3: e.matmul(
                            ps[0:N_SEQ, 512 * h:512 * h + 512], csil[:, k, :],
                            sbuf3[:, k, 512 * h:512 * h + 512],
                            start=(k == 0), stop=(k == NCHUNK - 1)),
                            reads=["csil", f"{skey}:{k}"], writes=wkey)
            else:
                for k in range(NCHUNK):
                    ctx.dma("G", f"wm{k % 2}", lambda e, k=k, jb=jb: e.dma_start(
                        ldb[k % len(ldb)][:],
                        dwmod[128 * k:128 * k + 128, HID * jb:HID * (jb + 1)]),
                        writes=[ldk[k % len(ldb)]])
                    for h in range(2):
                        wkey = [f"{pk}:{h}"] if k in (0, NCHUNK - 1) else []
                        top(lambda e, k=k, h=h, ps=ps: e.matmul(
                            ps[0:N_SEQ, 512 * h:512 * h + 512], csil[:, k, :],
                            ldb[k % len(ldb)][:, 512 * h:512 * h + 512],
                            start=(k == 0), stop=(k == NCHUNK - 1)),
                            reads=["csil", ldk[k % len(ldb)]], writes=wkey)
            vop(lambda e, ps=ps: e.tensor_copy(rb[0:N_SEQ, :], ps[0:N_SEQ, :]),
                reads=[f"{pk}:0", f"{pk}:1"], writes=[rbk])
            if jb in (1, 4):
                vop(lambda e: e.tensor_scalar_add(rb[0:N_SEQ, :], rb[0:N_SEQ, :], 1.0),
                    reads=[rbk], writes=[rbk])
            elif jb in (2, 5):
                vop(lambda e: e.tensor_scalar(rt_[0:N_SEQ, :], rb[0:N_SEQ, :],
                                              0.0, None, ALU.is_ge),
                    reads=[rbk], writes=[rtk])
                vop(lambda e: e.tensor_scalar(rt_[0:N_SEQ, :], rt_[0:N_SEQ, :],
                                              2.0, -1.0, ALU.mult, ALU.add),
                    reads=[rtk], writes=[rtk])
                vop(lambda e: e.tensor_mul(rb[0:N_SEQ, :], rb[0:N_SEQ, :],
                                           rt_[0:N_SEQ, :]),
                    reads=[rbk, rtk], writes=[rbk])
                vop(lambda e: e.tensor_scalar_max(rb[0:N_SEQ, :],
                                                  rb[0:N_SEQ, :], 0.25),
                    reads=[rbk], writes=[rbk])
                vop(lambda e: e.tensor_mul(rb[0:N_SEQ, :], rb[0:N_SEQ, :],
                                           rt_[0:N_SEQ, :]),
                    reads=[rbk, rtk], writes=[rbk])
            for n in range(N_SEQ):
                ctx.dma("G", "modw", lambda e, n=n, jb=jb: e.dma_start(
                    modscr[6 * n + jb:6 * n + jb + 1, :], rb[n:n + 1, :]),
                    reads=[rbk], writes=[f"modscr:{n}:{jb}"], batch=True)

    emit_mod([0, 1], None, None, rowbuf, "rowbuf",
             rowtmp, "rowtmp", stage=[(WVP, "WVPg"), (WVTS, "wvts_stage")])

    pst.close()
    ctx.fence_keys(["xb0", "xb1", "sqb", "lnb0", "lnb1", "hb0", "hb1"],
                   ["e:T", "e:V", "e:G", "d:wm0", "d:wm1", "d:modw"])

    WQ = A(nc.sbuf_tensor("WQ", [128, NCHUNK, HID], BF16, side="right"))
    WK = A(nc.sbuf_tensor("WK", [128, NCHUNK, HID], BF16, side="right"))
    WO = A(nc.sbuf_tensor("WO", [128, NCHUNK, HID], BF16, side="right"))
    H1T = A(nc.sbuf_tensor("H1T", [128, NCHUNK, seq], BF16, side="right"))
    VPT = A(nc.sbuf_tensor("VPT", [128, NCHUNK, seq], BF16, side="right"))

    for nm, sb, src in (("WQ", WQ, dwq), ("WK", WK, dwk)):
        ctx.dma("G", "wl", lambda e, sb=sb, src=src: e.dma_start(
            sb[:], src.rearrange("(k p) f -> p k f", p=128)),
            writes=[f"{nm}:{k}" for k in range(NCHUNK)], batch=True)

    qsb = [A(nc.sbuf_tensor(f"qsb{i}", [128, HID], BF16, side="right")) for i in range(1)]
    ksb = [A(nc.sbuf_tensor(f"ksb{i}", [128, HID], BF16, side="right")) for i in range(1)]
    vtm = [A(nc.sbuf_tensor(f"vtm{i}", [128, HID], BF16, side="right")) for i in range(1)]
    scrT = [A(nc.sbuf_tensor(f"scrT{i}", [128, NCHUNK, 128], BF16, side="right")) for i in range(1)]
    x1b = [A(nc.sbuf_tensor(f"x1b{i}", [128, HID], F32, side="right")) for i in range(NB2)]
    esb = A(nc.sbuf_tensor("esb", [128, 128], F32, side="right"))
    ca1rep = A(nc.sbuf_tensor("ca1rep", [128, HID], BF16))
    sarep = A(nc.sbuf_tensor("sarep", [128, HID], BF16))
    garep = A(nc.sbuf_tensor("garep", [128, HID], BF16))

    def emit_ffn_stats(xbuf, xkey, u):
        c = ffst[:, u, :]
        k = f"ffr:{u}"
        vop(lambda e, c=c: e.reduce_sum(c[:, 0:1], xbuf[:], axis=AX.X),
            reads=[xkey], writes=[k])
        vop(lambda e, c=c: e.scalar_tensor_tensor(
            lnb[u % 2][:], xbuf[:], 1.0, xbuf[:], ALU.mult, ALU.mult,
            accum_out=c[:, 1:2]),
            reads=[xkey], writes=[k, f"lnb{u % 2}"])

    def emit_rstd_batch(F, rk, k):
        vop(lambda e: e.tensor_scalar_mul(F[:, :, 0:1], F[:, :, 0:1], 1.0 / HID),
            reads=rk, writes=[k + ":mean"])
        vop(lambda e: e.tensor_mul(F[:, :, 2:3], F[:, :, 0:1], F[:, :, 0:1]),
            reads=[k + ":mean"], writes=[k + ":msq"])
        vop(lambda e: e.tensor_scalar(F[:, :, 1:2], F[:, :, 1:2], 1.0 / HID,
                                      EPS_LN, ALU.mult, ALU.add),
            reads=rk, writes=[k + ":v1"])
        vop(lambda e: e.tensor_sub(F[:, :, 1:2], F[:, :, 1:2], F[:, :, 2:3]),
            reads=[k + ":v1", k + ":msq"], writes=[k + ":var"])
        vop(lambda e: e.reciprocal(F[:, :, 3:4], F[:, :, 1:2]),
            reads=[k + ":var"], writes=[k + ":rvar"])
        sop(lambda e: e.activation(F[:, :, 3:4], F[:, :, 3:4], AF.Sqrt),
            reads=[k + ":rvar"], writes=[k + ":rstd"])
        vop(lambda e: e.scalar_tensor_tensor(
            F[:, :, 2:3], F[:, :, 0:1], -1.0, F[:, :, 3:4], ALU.mult, ALU.mult),
            reads=[k + ":mean", k + ":rstd"], writes=[k])

    def emit_ffn_batch(n):
        emit_rstd_batch(ffst[:, n * NT:(n + 1) * NT, :],
                        [f"ffr:{u}" for u in range(n * NT, (n + 1) * NT)],
                        f"ffb:{n}")

    xcs = [A(nc.sbuf_tensor(f"xc{i}", [128, HID], F32, side="right"))
           for i in range(2)]
    exstage = None

    def attn_bc(n):
        for rep, key, slot in ((ca1rep, "ca1rep", 1), (sarep, "sarep", 0)):
            ctx.dma("G", "bc" + key, lambda e, rep=rep, slot=slot, n=n: e.dma_start(
                rep[:],
                modscr[6 * n + slot:6 * n + slot + 1, :].to_broadcast([128, HID])),
                reads=[f"modscr:{n}:{slot}"], writes=[key])

    def a_fe_pre(n, t):
        bi = t % NB2
        ctx.dma("Y", f"xs{bi}", lambda e, t=t, bi=bi, n=n: e.dma_start(
            xb[bi][:], dx[n, 128 * t:128 * t + 128, :]),
            writes=[f"xb{bi}"])
        c = lnst[:, t, :]
        vop(lambda e, c=c, bi=bi: e.reduce_sum(c[:, 0:1], xb[bi][:], axis=AX.X),
            reads=[f"xb{bi}"], writes=[f"lnr:{t}"])
        vop(lambda e, c=c, bi=bi: e.scalar_tensor_tensor(
            lnb[t % 2][:], xb[bi][:], 1.0, xb[bi][:], ALU.mult, ALU.mult,
            accum_out=c[:, 1:2]),
            reads=[f"xb{bi}"], writes=[f"lnr:{t}", f"lnb{t % 2}"])

    def a_fe_post(n, t, t0):
        bi = t % NB2
        vop(lambda e, t=t, bi=bi: e.tensor_scalar(
            lnb[t % 2][:], xb[bi][:], lnst[:, t, 3:4], lnst[:, t, 2:3],
            ALU.mult, ALU.add),
            reads=[f"xb{bi}", f"lnp:{t0}"], writes=[f"lnb{t % 2}"])
        vop(lambda e, t=t: e.tensor_mul(sqb[:], lnb[t % 2][:], ca1rep[:]),
            reads=[f"lnb{t % 2}", "ca1rep"], writes=["sqb"])
        vop(lambda e, bi=bi: e.tensor_add(hb[bi][:], sqb[:], sarep[:]),
            reads=["sqb", "sarep"], writes=[f"hb{bi}"])
        ctx.dma("Y", f"h1w{bi}", lambda e, t=t, bi=bi, n=n: e.dma_start(
            h1scr[n, 128 * t:128 * t + 128, :], hb[bi][:]),
            reads=[f"hb{bi}"], writes=[f"h1s:{n}:{t}"])

    PB = NB2

    def a_fe(n, t):
        a_fe_pre(n, t)
        if (t + 1) % PB == 0:
            t0 = t - PB + 1
            emit_rstd_batch(lnst[:, t0:t0 + PB, :],
                            [f"lnr:{tt}" for tt in range(t0, t + 1)],
                            f"lnp:{t0}")
            for tt in range(t0, t + 1):
                a_fe_post(n, tt, t0)

    def a_rd(n, gg):
        for k in range(NCHUNK):
            ctx.dma("S", f"h1r{k}",
                    lambda e, k=k, gg=gg, n=n: e.dma_start(
                        H1T[:, k, GS * gg:GS * gg + GS],
                        h1scr[n, GS * gg:GS * gg + GS, 128 * k:128 * k + 128],
                        transpose=True),
                    reads=[f"h1s:{n}:{tt}"
                           for tt in range(GT * gg, GT * (gg + 1))],
                    writes=[f"h1Tg:{gg}:{k}"])

    def a_mm(n, t, qrot):
        gg = t // GT
        psQ, qk = ((psA, "psA") if (t % 2 == 0 or not qrot) else (psC, "psC"))
        for k in range(NCHUNK):
            for h in range(2):
                wb = [f"psB:{h}"] if k in (0, NCHUNK - 1) else []
                top(lambda e, k=k, h=h, t=t: e.matmul(
                    psB[:, 512 * h:512 * h + 512],
                    H1T[:, k, 128 * t:128 * t + 128],
                    WK[:, k, 512 * h:512 * h + 512],
                    start=(k == 0), stop=(k == NCHUNK - 1)),
                    reads=[f"h1Tg:{gg}:{k}", f"WK:{k}"], writes=wb)
        for k in range(NCHUNK):
            for h in range(2):
                wa = [f"{qk}:{h}"] if k in (0, NCHUNK - 1) else []
                top(lambda e, k=k, h=h, t=t, psQ=psQ: e.matmul(
                    psQ[:, 512 * h:512 * h + 512],
                    H1T[:, k, 128 * t:128 * t + 128],
                    WQ[:, k, 512 * h:512 * h + 512],
                    start=(k == 0), stop=(k == NCHUNK - 1)),
                    reads=[f"h1Tg:{gg}:{k}", f"WQ:{k}"], writes=wa)
        vop(lambda e: e.tensor_copy(ksb[0][:], psB[:]),
            reads=["psB:0", "psB:1"], writes=["ksb0"])
        vop(lambda e, psQ=psQ: e.tensor_copy(qsb[0][:], psQ[:]),
            reads=[f"{qk}:0", f"{qk}:1"], writes=["qsb0"])
        for blk in range(8):
            first = (t == 0 and blk == 0)
            last = (t == NT - 1 and blk == 7)
            top(lambda e, blk=blk, first=first, last=last: e.matmul(
                psE[:, :], qsb[0][:, 128 * blk:128 * blk + 128],
                ksb[0][:, 128 * blk:128 * blk + 128],
                start=first, stop=last),
                reads=["qsb0", "ksb0"],
                writes=(["psEacc"] if (first or last) else []))

    def mid(n):
        vop(lambda e: e.tensor_copy(esb[:], psE[:]),
            reads=["psEacc"], writes=["esb"])
        for dl in range(8):
            pk = "psC:0" if dl < 4 else "psC:1"
            top(lambda e, dl=dl: e.matmul(
                psC[0:16, 128 * dl:128 * dl + 128],
                id128f[:, 16 * dl:16 * dl + 16], esb[:],
                start=True, stop=True),
                reads=["esb", "id128f"], writes=[pk])
        vop(lambda e: e.tensor_copy(e16[0:16, 0:16], psC[0:16, 0:16]),
            reads=["psC:0"], writes=["e16"])
        for dl in range(1, 8):
            off = 128 * dl + 16 * dl
            vop(lambda e, off=off: e.tensor_add(
                e16[0:16, 0:16], e16[0:16, 0:16], psC[0:16, off:off + 16]),
                reads=["psC:0", "psC:1", "e16"], writes=["e16"])
        vop(lambda e: e.tensor_scalar_mul(e16[0:16, 0:16], e16[0:16, 0:16], 0.125),
            reads=["e16"], writes=["e16"])
        vop(lambda e: e.reduce_max(e16[0:16, 16:17], e16[0:16, 0:16], axis=AX.X,
                                   negate=True),
            reads=["e16"], writes=["e16nm"])
        vop(lambda e: e.tensor_scalar(e16[0:16, 32:48], e16[0:16, 0:16],
                                      e16[0:16, 16:17], None, ALU.add),
            reads=["e16", "e16nm"], writes=["e16s"])
        sop(lambda e: e.activation(e16[0:16, 32:48], e16[0:16, 32:48], AF.Exp),
            reads=["e16s"], writes=["e16s"])
        vop(lambda e: e.reduce_sum(e16[0:16, 17:18], e16[0:16, 32:48], axis=AX.X),
            reads=["e16s"], writes=["e16sum"])
        vop(lambda e: e.reciprocal(e16[0:16, 18:19], e16[0:16, 17:18]),
            reads=["e16sum"], writes=["e16rcp"])
        vop(lambda e: e.tensor_scalar(att_bf[:, :], e16[0:16, 32:48],
                                      e16[0:16, 18:19], None, ALU.mult),
            reads=["e16s", "e16rcp"], writes=["att_bf"])
        top(lambda e: e.transpose(psX[0:16, 0:16], att_bf[:, :],
                                  id128[0:16, 0:16]),
            reads=["att_bf", "id128"], writes=["psX:a"])
        vop(lambda e: e.tensor_copy(attT_sb[:], psX[0:16, 0:16]),
            reads=["psX:a"], writes=["attT_sb"])
        ctx.dma("G", "bdw", lambda e: e.dma_start(attscr, attT_sb[:]),
                reads=["attT_sb"], writes=["attscr"], batch=True)
        vop(lambda e: e.memset(bd8[:], 0.0), writes=["bd8"])
        for dl in range(8):
            ctx.dma("G", "bdr", lambda e, dl=dl: e.dma_start(
                bd8[16 * dl:16 * dl + 16, 16 * dl:16 * dl + 16], attscr),
                reads=["attscr"], writes=["bd8"], batch=True)
        for gph in range(NCHUNK):
            for h in range(2):
                top(lambda e, gph=gph, h=h: e.matmul(
                    psC[:, 512 * h:512 * h + 512], bd8[:],
                    WVTS[:, gph, 512 * h:512 * h + 512], start=True, stop=True),
                    reads=["bd8", f"WVTS:{gph}"], writes=[f"psC:{h}"])
            vop(lambda e, gph=gph: e.tensor_copy(WVTS[:, gph, :], psC[:]),
                reads=["psC:0", "psC:1"], writes=[f"WVTS:{gph}"])
            ctx.dma("G", "wvw", lambda e, gph=gph: e.dma_start(
                wvscr.rearrange("(a gg dl) f -> gg dl a f", a=16, gg=8)[gph],
                WVTS[:, gph, :]),
                reads=[f"WVTS:{gph}"], writes=[f"wvscr:{gph}"])
        for k in range(NCHUNK):
            ctx.dma("S", f"wvr{k}", lambda e, k=k: e.dma_start(
                WVP[:, k, :], wvscr[:, 128 * k:128 * k + 128], transpose=True),
                reads=[f"wvscr:{g}" for g in range(NCHUNK)],
                writes=[f"WVPg:{k}"])

    def b1_tile(n, t):
        bi = t % NB2
        gg = t // GT
        psV, vk = (psA, "psA") if t % 2 == 0 else (psB, "psB")
        for k in range(NCHUNK):
            for h in range(2):
                wkey = [f"{vk}:{h}"] if k in (0, NCHUNK - 1) else []
                top(lambda e, k=k, h=h, t=t, psV=psV: e.matmul(
                    psV[:, 512 * h:512 * h + 512],
                    H1T[:, k, 128 * t:128 * t + 128],
                    WVP[:, k, 512 * h:512 * h + 512],
                    start=(k == 0), stop=(k == NCHUNK - 1)),
                    reads=[f"h1Tg:{gg}:{k}", f"WVPg:{k}"],
                    writes=wkey)
        vop(lambda e, psV=psV: e.tensor_copy(vtm[0][:], psV[:]),
            reads=[f"{vk}:0", f"{vk}:1"], writes=["vtm0"])
        ctx.dma("Y", f"vw{t % 2}", lambda e, t=t: e.dma_start(
            vscr.rearrange("(j tg) f -> tg j f", j=16)[8 * t:8 * t + 8, :, :],
            vtm[0][:]),
            reads=["vtm0"], writes=[f"vscr:{t}"])

    def b1_vtr(n):
        for c in range(NCHUNK):
            ctx.dma("S", f"vtr{c}", lambda e, c=c: e.dma_start(
                VPT[:, c, :], vscr[:, 128 * c:128 * c + 128], transpose=True),
                reads=[f"vscr:{t}" for t in range(NT)],
                writes=[f"VPTg:{c}"])

    def b2_tile(n, rt, psO, ok, xbuf, xk, xsem):
        bi = rt % NB2
        for al in range(HPR):
            a = rt * HPR + al
            c = a // 2
            po = (a % 2) * 64
            srcv = VPT[po:po + 64, c, :].rearrange(
                "p (mj jl tg) -> p mj jl tg", mj=8, jl=2)
            for jl in range(2):
                vop(lambda e, srcv=srcv, jl=jl, al=al: e.tensor_copy(
                    scrT[0][64 * jl:64 * jl + 64, :,
                            T16 * al:T16 * al + T16],
                    srcv[:, :, jl, :]),
                    reads=[f"VPTg:{a // 2}"], writes=["scrT0"])
        for m in range(NCHUNK):
            for h in range(2):
                wkey = [f"{ok}:{h}"] if m == 0 else []
                top(lambda e, m=m, h=h, psO=psO: e.matmul(
                    psO[:, 512 * h:512 * h + 512], scrT[0][:, m, :],
                    WO[:, m, 512 * h:512 * h + 512],
                    start=(m == 0), stop=False),
                    reads=["scrT0", f"WO:{m}"], writes=wkey)
        for h in range(2):
            top(lambda e, h=h, psO=psO: e.matmul(
                psO[:, 512 * h:512 * h + 512], ones1[:],
                borow_bf[:, 512 * h:512 * h + 512], start=False, stop=True),
                reads=["ones1", "borow_bf"], writes=[f"{ok}:{h}"])
        ctx.dma("Y", xsem, lambda e, rt=rt, n=n, xbuf=xbuf: e.dma_start(
            xbuf[:], dx[n, 128 * rt:128 * rt + 128, :]),
            writes=[xk])
        vop(lambda e, bi=bi, psO=psO: e.tensor_mul(x1b[bi][:], psO[:],
                                                   garep[:]),
            reads=[f"{ok}:0", f"{ok}:1", "garep"], writes=[f"x1b{bi}"])
        vop(lambda e, bi=bi, xbuf=xbuf: e.tensor_add(x1b[bi][:], x1b[bi][:],
                                                     xbuf[:]),
            reads=[f"x1b{bi}", xk], writes=[f"x1b{bi}"])
        emit_ffn_stats(x1b[bi], f"x1b{bi}", n * NT + rt)
        ctx.dma("Y", f"x1w{bi}", lambda e, rt=rt, bi=bi, n=n: e.dma_start(
            x1scr[seq * n + 128 * rt: seq * n + 128 * rt + 128, :],
            x1b[bi][:]),
            reads=[f"x1b{bi}"], writes=[f"x1s:{n}:{rt}"])
        ctx.dma("Y", f"xow{bi}", lambda e, rt=rt, bi=bi, n=n: e.dma_start(
            dout[n, 128 * rt:128 * rt + 128, :], x1b[bi][:]),
            reads=[f"x1b{bi}"], writes=[f"outx:{n}:{rt}"])

    def bcgarep(n):
        ctx.dma("G", "bcgarep", lambda e, n=n: e.dma_start(
            garep[:],
            modscr[6 * n + 2:6 * n + 3, :].to_broadcast([128, HID])),
            reads=[f"modscr:{n}:2"], writes=["garep"])

    attn_bc(0)
    for step in range(NT + GT):
        if step < NT:
            a_fe(0, step)
        if step - GT >= 0:
            a_mm(0, step - GT, qrot=True)
        if step < NT and (step + 1) % GT == 0:
            a_rd(0, step // GT)
    ctx.dma("G", "wl2", lambda e: e.dma_start(
        WO[:], dwo.rearrange("(k p) f -> p k f", p=128)),
        writes=[f"WO:{k}" for k in range(NCHUNK)], batch=True)
    for g in range(NCHUNK):
        ctx.dma("G", f"wvt{g}", lambda e, g=g: e.dma_start(
            WVTS[:, g, :],
            dwvt.rearrange("(g p) f -> p g f", p=128)[:, g, :]),
            writes=[f"WVTS:{g}", f"wvts_stage:{g}"])
    mid(0)
    for t in range(NT):
        b1_tile(0, t)
    emit_mod([2, 3, 4, 5], hb, ["hb0", "hb1"], sqb, "sqb", lnb[0], "lnb0")
    b1_vtr(0)

    ZIP_MM = True
    bcgarep(0)
    attn_bc(1)
    for rt in range(NT):
        b2_tile(0, rt, psC, "psC", xcs[rt % 2], f"xc{rt % 2}",
                f"xr0{rt % 2}")
        a_fe(1, rt)
        if ZIP_MM and rt - GT >= 0:
            a_mm(1, rt - GT, qrot=False)
        if (rt + 1) % GT == 0:
            a_rd(1, rt // GT)
    emit_ffn_batch(0)
    for g in range(NCHUNK):
        ctx.dma("G", f"wvt{g}", lambda e, g=g: e.dma_start(
            WVTS[:, g, :],
            dwvt.rearrange("(g p) f -> p g f", p=128)[:, g, :]),
            writes=[f"WVTS:{g}", f"wvts_stage:{g}"])
    for t in range(0 if not ZIP_MM else NT - GT, NT):
        a_mm(1, t, qrot=(not ZIP_MM))
    mid(1)
    for t in range(NT):
        b1_tile(1, t)
    b1_vtr(1)
    bcgarep(1)
    for rt in range(NT):
        psO, ok = (psB, "psB") if rt % 2 == 0 else (psA, "psA")
        b2_tile(1, rt, psO, ok, xb[rt % NB2], f"xb{rt % NB2}", f"xr{rt % NB2}")
    emit_ffn_batch(1)

    ctx.barrier()

    ast_.close()
    EXPW = [E(nc.sbuf_tensor(f"expw{i}", [128, NCHUNK, HID], BF16))
            for i in range(NEXP)]
    NH2 = 3
    h2Tt = E(nc.sbuf_tensor("h2Tt", [128, NH2, NCHUNK, GS], BF16))
    h2T = [h2Tt[:, g, :, :] for g in range(NH2)]
    exp_views = [(EXPW[i].ap(), [f"ex:{i}"]) for i in range(NEXP)]

    cb1rep = E(nc.sbuf_tensor("cb1rep", [128, HID], BF16))
    sbrep2 = E(nc.sbuf_tensor("sbrep2", [128, HID], BF16))
    gbrep2 = E(nc.sbuf_tensor("gbrep2", [128, HID], BF16))

    def ffn_bcast(n, which):
        sel = {"fe": ((cb1rep, 4, "cb1rep"), (sbrep2, 3, "sbrep2")),
               "mm": ((gbrep2, 5, "gbrep2"),)}[which]
        for rep, slot, key in sel:
            ctx.dma("G", "bc2" + key, lambda e, rep=rep, slot=slot, n=n:
                    e.dma_start(rep[:], modscr[6 * n + slot:6 * n + slot + 1, :]
                                .to_broadcast([128, HID])),
                    writes=[key])
    ffn_bcast(0, "fe")
    ffn_bcast(0, "mm")
    for e_i in range(NEXP):
        ctx.dma("G", f"exl{e_i}", lambda e, e_i=e_i: e.dma_start(
            EXPW[e_i][:], dexp[e_i].rearrange("(k p) f -> p k f", p=128)),
            writes=[f"ex:{e_i}"])

    accb = [E(nc.sbuf_tensor(f"accb{i}", [128, HID], F32)) for i in range(NB2)]
    x1r = [E(nc.sbuf_tensor(f"x1r{i}", [128, HID], F32)) for i in range(2)]

    NGU = NU // GT

    def ffn_fe(u):
        gg = u // GT
        gbuf = gg % 2
        if True:
            n = u // NT
            bi = u % NB2
            xi = u % 2
            ctx.dma("Y", f"f1{xi}", lambda e, u=u, xi=xi: e.dma_start(
                x1r[xi][:], x1scr[128 * u:128 * u + 128, :]),
                writes=[f"x1r{xi}"])
            vop(lambda e, u=u, xi=xi: e.tensor_scalar(
                lnb[u % 2][:], x1r[xi][:], ffst[:, u, 3:4], ffst[:, u, 2:3],
                ALU.mult, ALU.add),
                reads=[f"x1r{xi}", f"ffb:{u // NT}"],
                writes=[f"lnb{u % 2}"])
            if u == NT:
                ffn_bcast(1, "fe")
            vop(lambda e, u=u: e.tensor_mul(sqb[:], lnb[u % 2][:], cb1rep[:]),
                reads=[f"lnb{u % 2}", "cb1rep"], writes=["sqb"])
            vop(lambda e, bi=bi: e.tensor_add(hb[bi][:], sqb[:], sbrep2[:]),
                reads=["sqb", "sbrep2"], writes=[f"hb{bi}"])
            ctx.dma("Y", f"h2w{bi}", lambda e, u=u, bi=bi: e.dma_start(
                h2scr[128 * u:128 * u + 128, :], hb[bi][:]),
                reads=[f"hb{bi}"], writes=[f"h2s:{u}"])
    def ffn_rd(gg):
        gbuf = gg % NH2
        for k in range(NCHUNK):
            ctx.dma("S", f"h2r{k}", lambda e, k=k, gg=gg, gbuf=gbuf:
                    e.dma_start(
                        h2T[gbuf][:, k, :],
                        h2scr[GS * gg:GS * gg + GS, 128 * k:128 * k + 128],
                        transpose=True),
                    reads=[f"h2s:{uu}" for uu in range(GT * gg, GT * (gg + 1))],
                    writes=[f"h2Tg:{gbuf}:{k}"])

    def ffn_mm(u):
        gg = u // GT
        gbuf = gg % NH2
        if True:
            n = u // NT
            t = u % NT
            bi = u % NB2
            toff = 128 * (u - GT * gg)
            if u == NT:
                ffn_bcast(1, "mm")
            for k in range(NCHUNK):
                wkey = ["psE:g"] if k in (0, NCHUNK - 1) else []
                top(lambda e, k=k, gbuf=gbuf, toff=toff: e.matmul(
                    psE[:, 0:8], h2T[gbuf][:, k, toff:toff + 128],
                    gatew[:, k, :], start=(k == 0), stop=(k == NCHUNK - 1)),
                    reads=[f"h2Tg:{gbuf}:{k}", "gatew"],
                    writes=wkey)
            vop(lambda e: e.tensor_add(gsc[:, 0:8], psE[:, 0:8],
                                       gbrep[:, 0:NEXP]),
                reads=["psE:g", "gbrep"], writes=["gp"])
            vop(lambda e: e.tensor_scalar_max(gsc[:, 0:8], gsc[:, 0:8], GATE_EPS),
                reads=["gp"], writes=["gp"])
            sop(lambda e: e.activation(gsc[:, 0:8], gsc[:, 0:8], AF.Exp),
                reads=["gp"], writes=["gp"])
            vop(lambda e: e.reduce_sum(gsc[:, 17:18], gsc[:, 0:8], axis=AX.X),
                reads=["gp"], writes=["gsum"])
            vop(lambda e: e.reciprocal(gsc[:, 18:19], gsc[:, 17:18]),
                reads=["gsum"], writes=["grcp"])
            vop(lambda e: e.tensor_scalar(gsc[:, 0:8], gsc[:, 0:8],
                                          gsc[:, 18:19], None, ALU.mult),
                reads=["gp", "grcp"], writes=["gp"])
            vop(lambda e: e.tensor_add(gsc[:, 48:56], gsc[:, 0:8],
                                       gbrep[:, NEXP:2 * NEXP]),
                reads=["gp", "gbrep"], writes=["gq"])
            vop(lambda e: e.max(gsc[:, 8:16], gsc[:, 48:56]),
                reads=["gq"], writes=["gtmp"])
            vop(lambda e: e.tensor_scalar(gsc[:, 24:32], gsc[:, 48:56],
                                          gsc[:, 9:10], None, ALU.is_ge),
                reads=["gq", "gtmp"], writes=["gmask"])
            vop(lambda e: e.tensor_scalar_max(gsc[:, 32:40], gsc[:, 0:8],
                                              W_THRES),
                reads=["gp"], writes=["gw"])
            vop(lambda e: e.tensor_mul(gsc[:, 32:40], gsc[:, 32:40],
                                       gsc[:, 24:32]),
                reads=["gw", "gmask"], writes=["gw"])
            vop(lambda e: e.reduce_sum(gsc[:, 18:19], gsc[:, 32:40], axis=AX.X),
                reads=["gw"], writes=["gtot"])
            vop(lambda e: e.reciprocal(gsc[:, 18:19], gsc[:, 18:19]),
                reads=["gtot"], writes=["gtot"])
            vop(lambda e: e.tensor_scalar(gsc[:, 40:48], gsc[:, 32:40],
                                          gsc[:, 18:19], 0.5, ALU.mult,
                                          ALU.mult),
                reads=["gw", "gtot"], writes=["comb"])
            acc = accb[bi]
            pss = [(psA, "psA"), (psB, "psB"), (psC, "psC")]
            for e_i in range(NEXP):
                ps, pk = pss[e_i % 3]
                view, keys = exp_views[e_i]
                for k in range(NCHUNK):
                    for h in range(2):
                        wkey = [f"{pk}:{h}"] if k == 0 else []
                        top(lambda e, k=k, h=h, ps=ps, view=view, gbuf=gbuf,
                            toff=toff: e.matmul(
                                ps[:, 512 * h:512 * h + 512],
                                h2T[gbuf][:, k, toff:toff + 128],
                                view[:, k, 512 * h:512 * h + 512],
                                start=(k == 0), stop=False),
                            reads=[f"h2Tg:{gbuf}:{k}"] + keys,
                            writes=wkey)
                bp = ballp[e_i // 4]
                po = 32 * (e_i % 4)
                for h in range(2):
                    top(lambda e, h=h, ps=ps, bp=bp, po=po: e.matmul(
                        ps[:, 512 * h:512 * h + 512],
                        onest[po:po + 1, :],
                        bp[po:po + 1, 512 * h:512 * h + 512],
                        start=False, stop=True,
                        tile_position=(po, 0)),
                        reads=["onest", f"ballp{e_i // 4}"],
                        writes=[f"{pk}:{h}"])
                if e_i == 0:
                    vop(lambda e, ps=ps, acc=acc: e.tensor_scalar(
                        acc[:], ps[:], gsc[:, 40:41], None, ALU.mult),
                        reads=["psA:0", "psA:1", "comb"], writes=[f"accb{bi}"])
                else:
                    vop(lambda e, ps=ps, e_i=e_i, acc=acc, pk=pk:
                        e.scalar_tensor_tensor(
                            acc[:], ps[:], gsc[:, 40 + e_i:41 + e_i], acc[:],
                            ALU.mult, ALU.add),
                        reads=[f"{pk}:0", f"{pk}:1", "comb", f"accb{bi}"],
                        writes=[f"accb{bi}"])
            vop(lambda e, acc=acc: e.tensor_mul(acc[:], acc[:], gbrep2[:]),
                reads=[f"accb{bi}", "gbrep2"], writes=[f"accb{bi}"])
            ctx.dma("G", f"ow{bi}", lambda e, u=u, n=n, t=t, acc=acc, bi=bi:
                    e.dma_start(dout[n, 128 * t:128 * t + 128, :], acc[:],
                                accum_op=ALU.add),
                    reads=[f"accb{bi}", f"outx:{n}:{t}"], writes=[f"out:{u}"])

    FLAG = 2 * GT
    for step in range(NU + FLAG):
        if step < NU:
            ffn_fe(step)
        if step - FLAG >= 0:
            ffn_mm(step - FLAG)
        if step < NU and (step + 1) % GT == 0:
            ffn_rd(step // GT)

    ctx.finish()
    st.close()
    return nc


def _stage_shared(Wq, Wk, Wv, Wo, bo, gate_W, gate_b, expert_W, expert_b, wmod):
    f32 = np.float32
    wq_p = np.ascontiguousarray(
        Wq.reshape(HID, HEADS, HD).transpose(0, 2, 1).reshape(HID, HID),
        dtype=f32)
    wk_p = np.ascontiguousarray(
        Wk.reshape(HID, HEADS, HD).transpose(0, 2, 1).reshape(HID, HID),
        dtype=f32)
    wvt_p = np.ascontiguousarray(
        Wv.T.reshape(HEADS, 8, 8, HID).transpose(1, 2, 0, 3).reshape(HID, HID),
        dtype=f32)
    return {
        "wq": wq_p, "wk": wk_p, "wvt": wvt_p,
        "wo": np.ascontiguousarray(Wo, dtype=f32),
        "bo": np.ascontiguousarray(np.asarray(bo).reshape(1, HID), dtype=f32),
        "gw": np.ascontiguousarray(gate_W, dtype=f32),
        "gbr": np.ascontiguousarray(
            np.broadcast_to(
                np.concatenate([
                    np.asarray(gate_b, dtype=np.float64).reshape(1, NEXP),
                    (np.arange(NEXP - 1, -1, -1, dtype=np.float64)
                     .reshape(1, NEXP) * 1e-16)], axis=1),
                (128, 2 * NEXP)),
            dtype=f32),
        "expw": np.ascontiguousarray(expert_W, dtype=f32),
        "ball": np.ascontiguousarray(expert_b, dtype=f32),
        "wmod": np.ascontiguousarray(wmod, dtype=f32),
        "ident": np.eye(128, dtype=f32),
    }


def kernel(L0_attn_x, L0_attn_Wq, L0_attn_Wk, L0_attn_Wv, L0_attn_Wo,
           L0_attn_bo, L0_ffn_gate_W, L0_ffn_gate_b, L0_ffn_expert_W,
           L0_ffn_expert_b, L0_c, L0_wmod):
    x = np.asarray(L0_attn_x)
    B, S, _ = x.shape
    shared = _stage_shared(
        np.asarray(L0_attn_Wq), np.asarray(L0_attn_Wk), np.asarray(L0_attn_Wv),
        np.asarray(L0_attn_Wo), np.asarray(L0_attn_bo),
        np.asarray(L0_ffn_gate_W), np.asarray(L0_ffn_gate_b),
        np.asarray(L0_ffn_expert_W), np.asarray(L0_ffn_expert_b),
        np.asarray(L0_wmod))
    cT = np.ascontiguousarray(np.asarray(L0_c).T, dtype=np.float32)

    nc = build_program(S)
    in_maps = []
    n_cores = B // N_SEQ
    for c in range(n_cores):
        m = dict(shared)
        m["x"] = np.ascontiguousarray(x[N_SEQ * c:N_SEQ * (c + 1)],
                                      dtype=np.float32)
        m["ct"] = np.ascontiguousarray(cT[:, N_SEQ * c:N_SEQ * (c + 1)])
        in_maps.append(m)
    res = run_bass_kernel_spmd(nc, in_maps, core_ids=list(range(n_cores)))
    out = np.concatenate([r["out"] for r in res.results], axis=0)
    return out.astype(np.float32)
